# version 21
# baseline (speedup 1.0000x reference)
import numpy as np
import concourse.bass as bass
import concourse.mybir as mybir
from concourse.bass_utils import run_bass_kernel_spmd
from contextlib import ExitStack

F32 = mybir.dt.float32
F32R = mybir.dt.float32r
BF16 = mybir.dt.bfloat16
I8 = mybir.dt.int8
AF = mybir.ActivationFunctionType
ALU = mybir.AluOpType

E_ = 2048
H_ = 2048
NH = 16
INTER = 8192
EPS = 1e-5
NCORES = 8

ENGS = ("sync", "tensor", "vector", "scalar", "gpsimd")


class Sem:
    def __init__(self, h):
        self.h = h
        self.n = 0

    def inc_(self, k):
        self.n += k
        return self.n


def build_nc(tok, p1_half, t2):
    assert tok % p1_half == 0 and tok % t2 == 0
    assert p1_half in (256, 512) and t2 % 128 == 0 and 256 <= t2 <= 512
    n_half = tok // p1_half
    n_t2 = tok // t2
    nsub1 = p1_half // 128
    nsub2 = t2 // 128
    nw1 = min(512, p1_half)
    nsl1 = p1_half // nw1
    ntb1 = p1_half // 128

    nc = bass.Bass("TRN2", target_bir_lowering=False, debug=False)

    x_d = nc.dram_tensor("x", [tok, E_], F32, kind="ExternalInput").ap()
    wqk_d = nc.dram_tensor("wqk", [E_, H_], F32, kind="ExternalInput").ap()
    wv_d = nc.dram_tensor("wv", [E_, H_], F32, kind="ExternalInput").ap()
    wu_d = nc.dram_tensor("wu", [E_, H_], F32, kind="ExternalInput").ap()
    wo_d = nc.dram_tensor("wo", [4 * 16 * 128, 512], F32, kind="ExternalInput").ap()
    bqk_d = nc.dram_tensor("bqk", [128, 16], F32, kind="ExternalInput").ap()
    bv_d = nc.dram_tensor("bv", [128, 16], F32, kind="ExternalInput").ap()
    bu_d = nc.dram_tensor("bu", [128, 16], F32, kind="ExternalInput").ap()
    bo_d = nc.dram_tensor("bo", [1, E_], F32, kind="ExternalInput").ap()
    lng_d = nc.dram_tensor("lng", [128, 16], F32, kind="ExternalInput").ap()
    lnb_d = nc.dram_tensor("lnb", [128, 16], F32, kind="ExternalInput").ap()
    wg_d = nc.dram_tensor("wg", [INTER, E_], I8, kind="ExternalInput").ap()
    wu8_d = nc.dram_tensor("wu8", [INTER, E_], I8, kind="ExternalInput").ap()
    wd_d = nc.dram_tensor("wd", [INTER * 2, 1024], I8, kind="ExternalInput").ap()
    sg_d = nc.dram_tensor("sg", [128, 64], F32, kind="ExternalInput").ap()
    su_d = nc.dram_tensor("su", [128, 64], F32, kind="ExternalInput").ap()
    sdr_d = nc.dram_tensor("sdr", [128, E_], F32, kind="ExternalInput").ap()
    id_d = nc.dram_tensor("ident", [128, 128], F32, kind="ExternalInput").ap()
    onec_d = nc.dram_tensor("onec", [128, 1], F32, kind="ExternalInput").ap()
    oner_d = nc.dram_tensor("oner", [1, 128], F32, kind="ExternalInput").ap()

    wop_d = nc.dram_tensor("wop", [4 * 16 * 128, 1024], BF16).ap()
    qsp_d = nc.dram_tensor("qsp", [H_, tok], F32).ap()
    usp_d = nc.dram_tensor("usp", [H_, tok], F32).ap()
    kvin_d = nc.dram_tensor("kvin", [128, NH * 128], F32)
    kvout_d = nc.dram_tensor("kvout", [128, NH * 128], F32)

    y_d = nc.dram_tensor("y", [tok, E_], F32, kind="ExternalOutput").ap()

    ops = {e: [] for e in ENGS}

    def E(eng, fn):
        ops[eng].append(fn)

    def W(eng, s, val=None):
        v = s.n if val is None else val
        if v > 0:
            E(eng, lambda e, h=s.h, v=v: e.wait_ge(h, v))

    def WP(eng, pair):
        if pair is not None:
            W(eng, pair[0], pair[1])

    def OP(eng, build, inc=None, k=1, dma=False):
        if not dma and eng == "vector":
            assert inc is None or inc is vE
            W(eng, vE, vE.n)
            inc = vE
        elif not dma and eng == "scalar":
            assert inc is None or inc is aE
            W(eng, aE, aE.n)
            inc = aE
        if inc is not None:
            inc.inc_(k)
            E(eng, lambda e, b=build, h=inc.h, k=k: b(e).then_inc(h, k))
        else:
            E(eng, lambda e, b=build: b(e))

    def DMA(eng, out, in_, s):
        OP(eng, (lambda e, o=out, i=in_: e.dma_start(out=o, in_=i)), inc=s,
           k=16, dma=True)

    stack = ExitStack()
    semnames = ("tE vE aE const x0 x1 wf0 wf1 wh0 wh1 spq spu kvb cc "
                "xq0 xq1 uq0 uq1 wof0 wof1 woh0 woh1 wgs0 wgs1 wus0 wus1 "
                "wds0 wds1 wds2 wds3 wds4 wds5 xq2 xq3 uq2 uq3 x2c "
                "yst x20 x21 spw0 spw1 gE "
                "wol0 wol1 wol2 wgs2 wus2 wohx2 wohx3 wohx4 wohx5 "
                "wolx3 wolx4 wolx5 wpa0 wpa1 wpb0 wpb1 yst1 x2b").split()
    sem = {n: Sem(stack.enter_context(nc.semaphore(n))) for n in semnames}
    tE, vE, aE = sem["tE"], sem["vE"], sem["aE"]

    PD = stack.enter_context(nc.psum_tensor("PD", [128, 2048], F32))
    PG = stack.enter_context(nc.psum_tensor("PG", [128, 512], F32))
    PU = stack.enter_context(nc.psum_tensor("PU", [128, 512], F32))
    PX = stack.enter_context(nc.psum_tensor("PX", [128, 512], F32))
    PY = stack.enter_context(nc.psum_tensor("PY", [128, 512], F32))
    TPB = (PG, PU)

    def sb(name, shape, dt, st=None):
        return (st or stack).enter_context(nc.sbuf_tensor(name, shape, dt))

    ident = sb("identb", [128, 128], F32)
    onec = sb("onecb", [128, 1], F32)
    oner = sb("onerb", [1, 128], F32)
    bqk_c = sb("bqkc", [128, 16], F32)
    bv_c = sb("bvc", [128, 16], F32)
    bu_c = sb("buc", [128, 16], F32)
    lng_c = sb("lngc", [128, 16], F32)
    lnb_c = sb("lnbc", [128, 16], F32)
    bo_r = sb("bor", [1, E_], F32)
    sg_c = sb("sgc", [128, 64], F32)
    su_c = sb("suc", [128, 64], F32)
    sdr_b = sb("sdrb", [128, E_], F32)
    kv_sb = sb("kvsb", [128, NH * 128], F32)

    sc = sem["const"]
    for dst, src in ((ident, id_d), (onec, onec_d), (oner, oner_d),
                     (bqk_c, bqk_d), (bv_c, bv_d), (bu_c, bu_d),
                     (lng_c, lng_d), (lnb_c, lnb_d), (bo_r, bo_d),
                     (sg_c, sg_d), (su_c, su_d), (sdr_b, sdr_d)):
        DMA("sync", dst[:, :], src, sc)
    for eng in ("tensor", "vector", "scalar"):
        W(eng, sc)

    p1 = ExitStack()
    n_h1buf = 2
    h1T_hi = [sb("h1Thi%d" % b, [128, 16 * p1_half], BF16, p1)
              for b in range(n_h1buf)]
    h1T_lo = [sb("h1Tlo%d" % b, [128, 16 * p1_half], BF16, p1)
              for b in range(n_h1buf)]
    h1Thi_vb = [t[:, :].rearrange("p (e t) -> p e t", e=16) for t in h1T_hi]
    h1Tlo_vb = [t[:, :].rearrange("p (e t) -> p e t", e=16) for t in h1T_lo]
    xb1 = [sb("xb1%d" % i, [128, E_], F32, p1) for i in range(3)]
    h1b = [sb("h1b%d" % i, [128, E_], F32, p1) for i in range(3)]
    j1 = sb("j1", [128, E_], F32, p1)
    j2 = sb("j2", [128, E_], F32, p1)
    statc = sb("statc", [128, 8], F32, p1)
    wfb = [sb("wf0b", [128, E_], F32, p1), sb("wf1b", [128, E_], F32, p1)]
    whb = [sb("wh0b", [128, E_], BF16, p1), sb("wh1b", [128, E_], BF16, p1)]
    wlb = [sb("wl0b", [128, E_], BF16, p1), sb("wl1b", [128, E_], BF16, p1)]
    wpf = [sb("wpf0", [128, 512], F32, p1)]
    wph = [sb("wph0", [128, 512], BF16, p1)]
    wpl = [sb("wpl0", [128, 512], BF16, p1)]
    qfm = sb("qfm", [128, p1_half], F32, p1)
    vfm = sb("vfm", [128, p1_half], F32, p1)
    ufm = sb("ufm", [128, p1_half], F32, p1)
    qtm = sb("qtm", [128, p1_half], F32, p1)
    vtm = sb("vtm", [128, p1_half], F32, p1)
    tmpa = sb("tmpa", [128, p1_half], F32, p1)
    tmpb = sb("tmpb", [128, p1_half], F32, p1)

    xsem = [sem["x0"], sem["x1"], sem["x2b"]]
    xfree = [None, None, None]
    wfsem = [sem["wf0"], sem["wf1"]]
    whsem = [sem["wh0"], sem["wh1"]]
    wfree = [None, None]
    wo_j = [0]

    wpfree = [None, None]

    def wo_presplit(n_slabs):
        for _ in range(n_slabs):
            j = wo_j[0]
            if j >= 64:
                return
            wo_j[0] += 1
            slot = 0
            if wpfree[slot] is not None:
                W("sync", *wpfree[slot])
                W("gpsimd", *wpfree[slot])
            srcw = wo_d[j * 128:(j + 1) * 128, :]
            wpa = sem["wpa0"]
            wpb = sem["wpb0"]
            DMA("sync", wpf[slot][:, :], srcw, wpa)
            DMA("gpsimd", wph[slot][:, :], srcw, wpb)
            W("gpsimd", wpa)
            W("gpsimd", wpb)
            OP("gpsimd", (lambda e, o=wpl[slot][:, :],
                          a=wpf[slot][:, :],
                          b=wph[slot][:, :]:
                          e.tensor_tensor(o, a, b, ALU.subtract)),
               inc=sem["gE"])
            W("gpsimd", sem["gE"], sem["gE"].n)
            spw = sem["spw0"]
            DMA("gpsimd", wop_d[j * 128:(j + 1) * 128, 0:512],
                wph[slot][:, :], spw)
            DMA("gpsimd", wop_d[j * 128:(j + 1) * 128, 512:1024],
                wpl[slot][:, :], spw)
            wpfree[slot] = (spw, spw.n)
    tp_freed = [None, None]
    tp_ctr = [0]
    bank_freed = [None, None, None, None]
    px_freed = [None]
    kv_seeded = [False] * NH
    h1_freed = [None, None, None]

    h1T_freed = [None, None]
    h1T_ready = [0, 0]

    h1_ready = {}

    def a_step_stats(half, s):
        off = half * p1_half
        slot = (half * nsub1 + s) % 3
        if xfree[slot] is not None:
            W("sync", *xfree[slot])
        DMA("sync", xb1[slot][:, :],
            x_d[off + s * 128: off + (s + 1) * 128, :], xsem[slot])
        xv = xsem[slot].n
        W("vector", xsem[slot], xv)
        OP("vector", (lambda e, i=xb1[slot][:, :]:
                      e.tensor_scalar(j1[:, :], i, 1.0, 0.0, ALU.mult,
                                      ALU.add,
                                      accum_out=statc[:, 0:1])), inc=vE)
        W("vector", vE)
        W("scalar", xsem[slot], xv)
        OP("scalar", (lambda e, i=xb1[slot][:, :]:
                      e.activation(j2[:, :], i, AF.Square,
                                   accum_out=statc[:, 1:2])), inc=aE)
        W("vector", aE)
        OP("vector", lambda e: e.tensor_scalar(
            statc[:, 2:3], statc[:, 0:1], 1.0 / E_, None, ALU.mult))
        OP("vector", lambda e: e.tensor_tensor(
            statc[:, 3:4], statc[:, 2:3], statc[:, 2:3], ALU.mult))
        OP("vector", lambda e: e.tensor_scalar(
            statc[:, 4:5], statc[:, 1:2], 1.0 / E_, None, ALU.mult))
        OP("vector", lambda e: e.tensor_tensor(
            statc[:, 3:4], statc[:, 4:5], statc[:, 3:4], ALU.subtract))
        OP("vector", lambda e: e.tensor_scalar(
            statc[:, 3:4], statc[:, 3:4], EPS, None, ALU.add))
        OP("vector", lambda e: e.reciprocal(
            statc[:, 4:5], statc[:, 3:4]), inc=vE)
        W("scalar", vE)
        OP("scalar", lambda e: e.activation(
            statc[:, 4:5], statc[:, 4:5], AF.Sqrt), inc=aE)
        hb = h1b[slot]
        WP("vector", h1_freed[slot])
        W("vector", aE)
        OP("vector", (lambda e, i=xb1[slot][:, :], o=hb[:, :]:
                      e.tensor_scalar(o, i, statc[:, 2:3], statc[:, 4:5],
                                      ALU.subtract, ALU.mult)), inc=vE)
        h1_v = vE.n
        xfree[slot] = (vE, h1_v)
        h1_ready[(half, s)] = h1_v

    def a_step_tp(half, s):
        buf = half % n_h1buf
        slot = (half * nsub1 + s) % 3
        hb = h1b[slot]
        h1_v = h1_ready[(half, s)]
        for eg in range(4):
            tpi = tp_ctr[0] % 2
            tp = TPB[tpi]
            W("tensor", vE, h1_v)
            WP("tensor", tp_freed[tpi])
            for k in range(4):
                OP("tensor",
                   (lambda e, o=tp[:, k * 128:(k + 1) * 128],
                    d=hb[:, (eg * 4 + k) * 128:(eg * 4 + k + 1) * 128],
                    st=(k == 0), sp=(k == 3):
                    e.matmul(o, d, ident[:, :], start=st, stop=sp,
                             is_transpose=True)),
                   inc=(tE if k == 3 else None))
            tv = tE.n
            dhi = h1Thi_vb[buf][:, eg * 4:(eg + 1) * 4, s * 128:(s + 1) * 128]
            dlo = h1Tlo_vb[buf][:, eg * 4:(eg + 1) * 4, s * 128:(s + 1) * 128]
            srcv = tp[:, :512].rearrange("p (b c) -> p b c", b=4)
            W("scalar", tE, tv)
            WP("scalar", h1T_freed[buf])
            OP("scalar", (lambda e, o=dhi, i=srcv:
                          e.activation(o, i, AF.Copy)), inc=aE)
            av = aE.n
            W("vector", aE, av)
            OP("vector", (lambda e, o=dlo, i=srcv, h2=dhi:
                          e.tensor_tensor(o, i, h2, ALU.subtract)), inc=vE)
            tp_freed[tpi] = (vE, vE.n)
            tp_ctr[0] += 1
        h1_freed[slot] = (vE, vE.n)
        if s == nsub1 - 1:
            h1T_ready[buf] = vE.n
        if half < 2:
            wo_presplit((64 // (min(2, n_half) * nsub1)) +
                        (64 % (min(2, n_half) * nsub1) if
                         (half == min(2, n_half) - 1 and s == nsub1 - 1)
                         else 0))

    def a_step_sub(half, s):
        a_step_stats(half, s)
        a_step_tp(half, s)

    def w_loop(half):
        buf = half % n_h1buf
        off = half * p1_half
        h1Thi_v = h1Thi_vb[buf]
        h1Tlo_v = h1Tlo_vb[buf]
        projs = (("q", wqk_d, bqk_c), ("v", wv_d, bv_c), ("u", wu_d, bu_c))

        def w_load(gl):
            m2, pi2 = divmod(gl, 3)
            slot = (half * 48 + gl) % 2
            wsrc = projs[pi2][1]
            if wfree[slot] is not None:
                W("sync", *wfree[slot])
                W("gpsimd", *wfree[slot])
            DMA("sync", wfb[slot][:, :],
                wsrc[m2 * 128:(m2 + 1) * 128, :], wfsem[slot])
            DMA("gpsimd", whb[slot][:, :],
                wsrc[m2 * 128:(m2 + 1) * 128, :], whsem[slot])

        def w_sub(gl):
            slot = (half * 48 + gl) % 2
            W("vector", wfsem[slot])
            W("vector", whsem[slot])
            OP("vector", (lambda e, o=wlb[slot][:, :], a=wfb[slot][:, :],
                          b=whb[slot][:, :]:
                          e.tensor_tensor(o, a, b, ALU.subtract)), inc=vE)
            sub_v[slot] = vE.n

        sub_v = [0, 0]
        w_load(0)
        w_load(1)
        w_sub(0)
        eps = {}
        hooks = {}
        if n_h1buf == 2 and half + 1 < n_half:
            for s2 in range(nsub1):
                hooks[min(12 + s2 * max(1, 32 // nsub1), 40)] = \
                    ("full", half + 1, s2)
        elif half + 1 < n_half:
            hooks[40] = ("stats", half + 1, 0)
            hooks[44] = ("stats", half + 1, 1)
        for gl in range(48):
            if gl in hooks:
                kind2, h2_, s2_ = hooks[gl]
                if kind2 == "full":
                    a_step_sub(h2_, s2_)
                else:
                    a_step_stats(h2_, s2_)
            m, pi = divmod(gl, 3)
            kind, wsrc, bias = projs[pi]
            g = half * 48 + gl
            slot = g % 2
            bank = g % 4
            pp = PD[:, bank * 512: bank * 512 + 512]
            if gl + 1 < 48:
                w_sub(gl + 1)
            W("tensor", vE, max(sub_v[slot], h1T_ready[buf]))
            WP("tensor", bank_freed[bank])
            first = True
            for p_i, (arhs, wmat) in enumerate(
                    ((h1Thi_v, whb[slot]), (h1Tlo_v, whb[slot]),
                     (h1Thi_v, wlb[slot]))):
                for e_c in range(16):
                    for ns in range(nsl1):
                        last = (p_i == 2 and e_c == 15 and ns == nsl1 - 1)
                        OP("tensor",
                           (lambda e, o=pp[:, ns * nw1:(ns + 1) * nw1],
                            lh=wmat[:, e_c * 128:(e_c + 1) * 128],
                            rh=arhs[:, e_c, ns * nw1:(ns + 1) * nw1],
                            st=first, sp=last:
                            e.matmul(o, lh, rh, start=st, stop=sp)),
                           inc=(tE if last else None))
                        first = False
            mm_v = tE.n
            wfree[slot] = (tE, mm_v)
            if gl + 2 < 48:
                w_load(gl + 2)
            if kind == "q":
                W("vector", tE, mm_v)
                W("vector", sem["spq"], sem["spq"].n)
                OP("vector", (lambda e, b=bias[:, m:m + 1], p=pp:
                              e.tensor_scalar(tmpa[:, :], p[:, :p1_half],
                                              b, None, ALU.add)))
                OP("vector", lambda e: e.tensor_scalar(
                    tmpb[:, :], tmpa[:, :], 0.0, None, ALU.min), inc=vE)
                W("scalar", vE)
                OP("scalar", lambda e: e.activation(
                    tmpb[:, :], tmpb[:, :], AF.Exp), inc=aE)
                OP("vector", lambda e: e.tensor_scalar(
                    qfm[:, :], tmpa[:, :], 0.0, -1.0, ALU.max, ALU.add))
                W("vector", aE)
                OP("vector", lambda e: e.tensor_tensor(
                    qfm[:, :], qfm[:, :], tmpb[:, :], ALU.add), inc=vE)
                ep = (vE, vE.n)
                bank_freed[bank] = ep
                W("gpsimd", vE, ep[1])
                DMA("gpsimd",
                    qsp_d[m * 128:(m + 1) * 128, off:off + p1_half],
                    qfm[:, :], sem["spq"])
                eps["q"] = ep
            elif kind == "v":
                W("scalar", tE, mm_v)
                OP("scalar", (lambda e, b=bias[:, m:m + 1], p=pp:
                              e.activation(vfm[:, :], p[:, :p1_half],
                                           AF.Silu, bias=b, scale=1.0)),
                   inc=aE)
                ep = (aE, aE.n)
                bank_freed[bank] = ep
                eps["v"] = ep
            else:
                W("scalar", tE, mm_v)
                W("scalar", sem["spu"], sem["spu"].n)
                OP("scalar", (lambda e, b=bias[:, m:m + 1], p=pp:
                              e.activation(ufm[:, :], p[:, :p1_half],
                                           AF.Silu, bias=b, scale=1.0)),
                   inc=aE)
                ep = (aE, aE.n)
                bank_freed[bank] = ep
                W("gpsimd", aE, ep[1])
                DMA("gpsimd",
                    usp_d[m * 128:(m + 1) * 128, off:off + p1_half],
                    ufm[:, :], sem["spu"])
                for src_fm, dst_tm, epk in ((qfm, qtm, "q"), (vfm, vtm, "v")):
                    for tb0 in range(0, ntb1, 4):
                        nblk = min(4, ntb1 - tb0)
                        tpi = tp_ctr[0] % 2
                        tp = TPB[tpi]
                        WP("tensor", eps[epk])
                        WP("tensor", tp_freed[tpi])
                        for k in range(nblk):
                            OP("tensor",
                               (lambda e, o=tp[:, k * 128:(k + 1) * 128],
                                d=src_fm[:, (tb0 + k) * 128:
                                         (tb0 + k + 1) * 128],
                                st=(k == 0), sp=(k == nblk - 1):
                                e.matmul(o, d, ident[:, :], start=st, stop=sp,
                                         is_transpose=True)),
                               inc=(tE if k == nblk - 1 else None))
                        tv = tE.n
                        W("vector", tE, tv)
                        OP("vector",
                           (lambda e, o=dst_tm[:, tb0 * 128:
                                               tb0 * 128 + nblk * 128],
                            i=tp[:, :nblk * 128]:
                            e.tensor_copy(o, i)), inc=vE)
                        tp_freed[tpi] = (vE, vE.n)
                kv_in_v = vE.n
                W("tensor", vE, kv_in_v)
                WP("tensor", px_freed[0])
                for b in range(ntb1):
                    OP("tensor",
                       (lambda e, lh=qtm[:, b * 128:(b + 1) * 128],
                        rh=vtm[:, b * 128:(b + 1) * 128],
                        st=(b == 0), sp=(b == ntb1 - 1):
                        e.matmul(PX[:, :128], lh, rh, start=st, stop=sp)),
                       inc=(tE if b == ntb1 - 1 else None))
                kvmm_v = tE.n
                W("vector", tE, kvmm_v)
                if not kv_seeded[m]:
                    OP("vector", (lambda e, o=kv_sb[:, m * 128:(m + 1) * 128]:
                                  e.tensor_copy(o, PX[:, :128])), inc=vE)
                    kv_seeded[m] = True
                else:
                    OP("vector", (lambda e, o=kv_sb[:, m * 128:(m + 1) * 128]:
                                  e.tensor_tensor(o, o, PX[:, :128], ALU.add)),
                       inc=vE)
                px_freed[0] = (vE, vE.n)
        h1T_freed[buf] = (tE, tE.n)

    for s0 in range(nsub1):
        a_step_sub(0, s0)
    for half in range(n_half):
        w_loop(half)
        if n_h1buf == 1 and half + 1 < n_half:
            for s0 in range(nsub1):
                if (half + 1, s0) in h1_ready:
                    a_step_tp(half + 1, s0)
                else:
                    a_step_sub(half + 1, s0)

    spq_done = sem["spq"].n
    spu_done = sem["spu"].n
    spw_done = (sem["spw0"].n, sem["spw1"].n)
    assert wo_j[0] == 64
    W("gpsimd", vE)
    DMA("gpsimd", kvin_d.ap(), kv_sb[:, :], sem["kvb"])
    W("gpsimd", sem["kvb"])
    OP("gpsimd", lambda e: e.collective_compute(
        "AllReduce", ALU.add, replica_groups=[[0, 1, 2, 3], [4, 5, 6, 7]],
        ins=[kvin_d.ap()], outs=[kvout_d.ap()]), inc=sem["cc"])
    W("gpsimd", sem["cc"])
    DMA("gpsimd", kv_sb[:, :], kvout_d.ap(), sem["kvb"])
    kv_final = sem["kvb"].n

    p1t, p1v, p1a = tE.n, vE.n, aE.n
    for eng in ENGS:
        W(eng, tE, p1t)
        W(eng, vE, p1v)
        W(eng, aE, p1a)
        W(eng, sem["spq"], spq_done)
        W(eng, sem["spu"], spu_done)
        W(eng, sem["spw0"], spw_done[0])
        W(eng, sem["spw1"], spw_done[1])
        W(eng, sem["kvb"], kv_final)

    p1.close()

    p2 = ExitStack()
    xb2 = [sb("xb2a", [128, nsub2 * E_], F32, p2),
           sb("xb2b", [128, nsub2 * E_], F32, p2)]
    qhb = [sb("qh%d" % i, [128, t2], F32, p2) for i in range(4)]
    uhb = [sb("uh%d" % i, [128, t2], F32, p2) for i in range(4)]
    outsb = sb("outsb", [128, NH * t2], F32, p2)
    sqb = sb("sqb", [128, t2], F32, p2)
    rowb = sb("rowb", [1, 5 * t2], F32, p2)
    afb2 = [sb("afb%d" % i, [128, t2], F32, p2) for i in range(2)]
    af22 = [sqb, sb("af2b", [128, t2], F32, p2)]
    arena_hi = sb("arenahi", [128, NH * t2], BF16, p2)
    arena_lo = sb("arenalo", [128, NH * t2], BF16, p2)
    h2T_hi_t = sb("h2thi", [128, NH * t2], BF16, p2)
    h2T_lo_t = sb("h2tlo", [128, NH * t2], BF16, p2)
    h2Thi_v = h2T_hi_t[:, :].rearrange("p (e t) -> p e t", e=16)
    h2Tlo_v = h2T_lo_t[:, :].rearrange("p (e t) -> p e t", e=16)
    wopb = [sb("wop%d" % i, [128, 1024], BF16, p2) for i in range(6)]
    h2b = sb("h2b", [128, E_], F32, p2)
    st2 = sb("st2", [128, 8], F32, p2)
    wgb = [sb("wg%db" % i, [128, E_], BF16, p2) for i in range(3)]
    wub = [sb("wu%db" % i, [128, E_], BF16, p2) for i in range(3)]
    wdb = [sb("wd%db" % i, [128, 1024], BF16, p2) for i in range(6)]
    gsb = sb("gsb", [128, t2], F32, p2)
    usb = sb("usb", [128, t2], F32, p2)
    ffb = sb("ffb", [128, t2], F32, p2)
    fghi = sb("fghi", [128, 8 * t2], BF16, p2)
    fglo = sb("fglo", [128, 8 * t2], BF16, p2)
    assert NH * t2 == nsub2 * E_
    accf = outsb

    x2sem = [sem["x20"], sem["x21"]]
    qsem = [sem["xq0"], sem["xq1"], sem["xq2"], sem["xq3"]]
    usem = [sem["uq0"], sem["uq1"], sem["uq2"], sem["uq3"]]
    wos4 = [sem["wof0"], sem["wof1"], sem["woh0"], sem["woh1"],
            sem["wohx2"], sem["wohx3"]]
    wgsm = [sem["wgs0"], sem["wgs1"], sem["wgs2"]]
    wusm = [sem["wus0"], sem["wus1"], sem["wus2"]]
    wdsm = [sem["wds0"], sem["wds1"], sem["wds2"], sem["wds3"],
            sem["wds4"], sem["wds5"]]

    qfree = [None] * 4
    ufree = [None] * 4
    wofree = [None] * 6
    wgfree = [None, None, None]
    wufree = [None, None, None]
    wdfree = [None] * 6
    x2free = [None, None]
    xst_free = [None, None]
    ystp = [sem["yst"], sem["yst1"]]
    oev_freed = [None, None]
    wo_bank_freed = [None] * 4
    pd_down_freed = None
    fuse_freed = None
    gu_freed = [None, None]
    h2tp_freed = [None, None]
    stats_freed = None
    h2_freed = None


    qctr = 0
    woctr = 0
    tpctr2 = 0
    gsb_freed = None
    ffb_freed = None

    for t in range(n_t2):
        toff = t * t2
        xslot = t % 2
        xcur = xb2[xslot]
        WP("sync", x2free[xslot])
        WP("sync", xst_free[xslot])
        for s in range(nsub2):
            DMA("sync", xcur[:, s * E_:(s + 1) * E_],
                x_d[toff + s * 128: toff + (s + 1) * 128, :], x2sem[xslot])
        xv2 = x2sem[xslot].n

        WP("tensor", stats_freed)
        WP("tensor", gu_freed[0])
        WP("tensor", gu_freed[1])
        WP("tensor", h2tp_freed[0])
        WP("tensor", h2tp_freed[1])
        W("scalar", vE, vE.n)
        for h in range(NH):
            slot = qctr % 4
            WP("scalar", qfree[slot])
            DMA("scalar", qhb[slot][:, :],
                qsp_d[h * 128:(h + 1) * 128, toff:toff + t2], qsem[slot])
            qv = qsem[slot].n
            tpi = h % 2
            tp = TPB[tpi]
            W("tensor", qsem[slot], qv)
            WP("tensor", oev_freed[tpi])
            OP("tensor", (lambda e, lh=kv_sb[:, h * 128:(h + 1) * 128],
                          rh=qhb[slot][:, :], o=tp[:, :t2]:
                          e.matmul(o, lh, rh, start=True, stop=True)), inc=tE)
            ov = tE.n
            qfree[slot] = (tE, ov)
            W("scalar", tE, ov)
            OP("scalar", (lambda e, o=outsb[:, h * t2:(h + 1) * t2],
                          i=tp[:, :t2]: e.activation(o, i, AF.Copy)), inc=aE)
            OP("scalar", (lambda e, i=tp[:, :t2]:
                          e.activation(sqb[:, :], i, AF.Square)), inc=aE)
            sq_v = aE.n
            oev_freed[tpi] = (aE, sq_v)
            qctr += 1
            W("tensor", aE, sq_v)
            OP("tensor", (lambda e, rh=outsb[:, h * t2:(h + 1) * t2],
                          st=(h == 0), sp=(h == NH - 1):
                          e.matmul(PX[:1, :t2], onec[:, :], rh,
                                   start=st, stop=sp)),
               inc=(tE if h == NH - 1 else None))
            OP("tensor", (lambda e, st=(h == 0), sp=(h == NH - 1):
                          e.matmul(PY[:1, :t2], onec[:, :], sqb[:, :],
                                   start=st, stop=sp)),
               inc=(tE if h == NH - 1 else None))
        stats_v = tE.n
        R = rowb
        W("vector", tE, stats_v)
        OP("vector", lambda e: e.tensor_scalar(
            R[:1, 0 * t2:1 * t2], PX[:1, :t2], 1.0 / H_, None, ALU.mult))
        OP("vector", lambda e: e.tensor_scalar(
            R[:1, 1 * t2:2 * t2], PY[:1, :t2], 1.0 / H_, None, ALU.mult))
        OP("vector", lambda e: e.tensor_tensor(
            R[:1, 2 * t2:3 * t2], R[:1, 0 * t2:1 * t2], R[:1, 0 * t2:1 * t2],
            ALU.mult))
        OP("vector", lambda e: e.tensor_tensor(
            R[:1, 2 * t2:3 * t2], R[:1, 1 * t2:2 * t2], R[:1, 2 * t2:3 * t2],
            ALU.subtract))
        OP("vector", lambda e: e.tensor_scalar(
            R[:1, 2 * t2:3 * t2], R[:1, 2 * t2:3 * t2], EPS, None, ALU.add))
        OP("vector", lambda e: e.reciprocal(
            R[:1, 3 * t2:4 * t2], R[:1, 2 * t2:3 * t2]), inc=vE)
        W("scalar", vE)
        OP("scalar", lambda e: e.activation(
            R[:1, 3 * t2:4 * t2], R[:1, 3 * t2:4 * t2], AF.Sqrt), inc=aE)
        W("vector", aE)
        OP("vector", lambda e: e.tensor_tensor(
            R[:1, 4 * t2:5 * t2], R[:1, 0 * t2:1 * t2], R[:1, 3 * t2:4 * t2],
            ALU.mult), inc=vE)
        rows_v = vE.n
        W("tensor", vE, rows_v)
        OP("tensor", lambda e: e.matmul(
            PX[:, :t2], oner[:, :], R[:1, 3 * t2:4 * t2],
            start=True, stop=True), inc=tE)
        OP("tensor", lambda e: e.matmul(
            PY[:, :t2], oner[:, :], R[:1, 4 * t2:5 * t2],
            start=True, stop=True), inc=tE)
        bc_v = tE.n

        a_vh = [0] * NH
        afb_freed = [None, None]
        for h in range(NH):
            slot = h % 4
            WP("scalar", ufree[slot])
            DMA("scalar", uhb[slot][:, :],
                usp_d[h * 128:(h + 1) * 128, toff:toff + t2], usem[slot])
            uv = usem[slot].n
            afb = afb2[h % 2]
            af2 = af22[h % 2]
            W("vector", tE, bc_v)
            W("vector", usem[slot], uv)
            WP("vector", afb_freed[h % 2])
            OP("vector", (lambda e, i=outsb[:, h * t2:(h + 1) * t2], o=afb:
                          e.tensor_tensor(o[:, :], i, PX[:, :t2], ALU.mult)))
            OP("vector", (lambda e, o=afb: e.tensor_tensor(
                o[:, :], o[:, :], PY[:, :t2], ALU.subtract)), inc=vE)
            W("scalar", vE, vE.n)
            OP("scalar", (lambda e, gg=lng_c[:, h:h + 1], bb=lnb_c[:, h:h + 1],
                          o=af2, i=afb:
                          e.activation(o[:, :], i[:, :], AF.Identity,
                                       bias=bb, scale=gg)), inc=aE)
            W("vector", aE, aE.n)
            OP("vector", (lambda e, u=uhb[slot][:, :], o=af2:
                          e.tensor_tensor(o[:, :], o[:, :], u, ALU.mult)),
               inc=vE)
            ufree[slot] = (vE, vE.n)
            W("scalar", vE, vE.n)
            OP("scalar", (lambda e, o=arena_hi[:, h * t2:(h + 1) * t2], i=af2:
                          e.activation(o, i[:, :], AF.Copy)), inc=aE)
            afb_hi_a = aE.n
            W("vector", aE, aE.n)
            OP("vector", (lambda e, o=arena_lo[:, h * t2:(h + 1) * t2],
                          hh=arena_hi[:, h * t2:(h + 1) * t2],
                          i=af2:
                          e.tensor_tensor(o, i[:, :], hh, ALU.subtract)),
               inc=vE)
            a_vh[h] = vE.n
            afb_freed[h % 2] = (vE, vE.n)
        a_v = vE.n
        stats_freed = (vE, a_v)

        for sl in range(4):
            for h in range(NH):
                slot = woctr % 6
                WP("sync", wofree[slot])
                DMA("sync", wopb[slot][:, :],
                    wop_d[(sl * 16 + h) * 128:(sl * 16 + h + 1) * 128, :],
                    wos4[slot])
                W("tensor", wos4[slot])
                W("tensor", vE, a_vh[h])
                if h == 0:
                    WP("tensor", pd_down_freed)
                    WP("tensor", wo_bank_freed[(sl % 2) * 2])
                    WP("tensor", wo_bank_freed[(sl % 2) * 2 + 1])
                last_of_slab = None
                for sub in range(nsub2):
                    bank = (sl % 2) * nsub2 + sub
                    pp = PD[:, bank * 512:(bank + 1) * 512]
                    if h == 0:
                        OP("tensor", (lambda e, o=pp,
                                      rh=bo_r[:1, sl * 512:(sl + 1) * 512]:
                                      e.matmul(o, oner[:, :], rh,
                                               start=True, stop=False)))
                    for pi2, (amat, wmat) in enumerate(
                            ((arena_hi, wopb[slot][:, 0:512]),
                             (arena_lo, wopb[slot][:, 0:512]),
                             (arena_hi, wopb[slot][:, 512:1024]))):
                        sp = (h == NH - 1 and pi2 == 2)
                        lastg = (sp and sub == nsub2 - 1)
                        OP("tensor",
                           (lambda e, o=pp,
                            lh=amat[:, h * t2 + sub * 128:
                                    h * t2 + (sub + 1) * 128],
                            rh=wmat[:, :], sp=sp:
                            e.matmul(o, lh, rh, start=False, stop=sp)),
                           inc=(tE if (lastg or (pi2 == 2 and
                                                 sub == nsub2 - 1)) else None))
                wofree[slot] = (tE, tE.n)
                woctr += 1
            grp_v = tE.n
            W("vector", tE, grp_v)
            W("vector", x2sem[xslot], xv2)
            for sub in range(nsub2):
                bank = (sl % 2) * nsub2 + sub
                pp = PD[:, bank * 512:(bank + 1) * 512]
                xs = xcur[:, sub * E_ + sl * 512: sub * E_ + (sl + 1) * 512]
                OP("vector", (lambda e, o=xs, a=xs, b=pp:
                              e.tensor_tensor(o, a, b, ALU.add)), inc=vE)
                wo_bank_freed[bank] = (vE, vE.n)
        xnew_v = vE.n
        x2free[xslot] = (vE, xnew_v)

        for s in range(nsub2):
            xs = xcur[:, s * E_:(s + 1) * E_]
            W("vector", vE, xnew_v)
            OP("vector", (lambda e, i=xs:
                          e.tensor_scalar(accf[:, 0:E_], i, 1.0, 0.0,
                                          ALU.mult, ALU.add,
                                          accum_out=st2[:, 0:1])), inc=vE)
            W("vector", vE)
            W("scalar", vE, xnew_v)
            OP("scalar", (lambda e, i=xs:
                          e.activation(accf[:, E_:2 * E_], i, AF.Square,
                                       accum_out=st2[:, 1:2])), inc=aE)
            W("vector", aE)
            OP("vector", lambda e: e.tensor_scalar(
                st2[:, 2:3], st2[:, 0:1], 1.0 / E_, None, ALU.mult))
            OP("vector", lambda e: e.tensor_tensor(
                st2[:, 3:4], st2[:, 2:3], st2[:, 2:3], ALU.mult))
            OP("vector", lambda e: e.tensor_scalar(
                st2[:, 4:5], st2[:, 1:2], 1.0 / E_, None, ALU.mult))
            OP("vector", lambda e: e.tensor_tensor(
                st2[:, 3:4], st2[:, 4:5], st2[:, 3:4], ALU.subtract))
            OP("vector", lambda e: e.tensor_scalar(
                st2[:, 3:4], st2[:, 3:4], EPS, None, ALU.add))
            OP("vector", lambda e: e.reciprocal(
                st2[:, 4:5], st2[:, 3:4]), inc=vE)
            W("scalar", vE)
            OP("scalar", lambda e: e.activation(
                st2[:, 4:5], st2[:, 4:5], AF.Sqrt), inc=aE)
            WP("vector", h2_freed)
            W("vector", aE)
            OP("vector", (lambda e, i=xs:
                          e.tensor_scalar(h2b[:, :], i, st2[:, 2:3],
                                          st2[:, 4:5], ALU.subtract,
                                          ALU.mult)), inc=vE)
            h2_v = vE.n
            for eg in range(4):
                tpi = tpctr2 % 2
                tp = TPB[tpi]
                W("tensor", vE, h2_v)
                WP("tensor", h2tp_freed[tpi])
                WP("tensor", oev_freed[tpi])
                oev_freed[tpi] = None
                WP("tensor", gu_freed[tpi])
                gu_freed[tpi] = None
                for k in range(4):
                    OP("tensor",
                       (lambda e, o=tp[:, k * 128:(k + 1) * 128],
                        d=h2b[:, (eg * 4 + k) * 128:(eg * 4 + k + 1) * 128],
                        st=(k == 0), sp=(k == 3):
                        e.matmul(o, d, ident[:, :], start=st, stop=sp,
                                 is_transpose=True)),
                       inc=(tE if k == 3 else None))
                tv = tE.n
                dhi = h2Thi_v[:, eg * 4:(eg + 1) * 4, s * 128:(s + 1) * 128]
                dlo = h2Tlo_v[:, eg * 4:(eg + 1) * 4, s * 128:(s + 1) * 128]
                srcv = tp[:, :512].rearrange("p (b c) -> p b c", b=4)
                W("scalar", tE, tv)
                OP("scalar", (lambda e, o=dhi, i=srcv:
                              e.activation(o, i, AF.Copy)), inc=aE)
                W("vector", aE)
                OP("vector", (lambda e, o=dlo, i=srcv, hh=dhi:
                              e.tensor_tensor(o, i, hh, ALU.subtract)), inc=vE)
                h2tp_freed[tpi] = (vE, vE.n)
                tpctr2 += 1
            h2_freed = (vE, vE.n)
        h2T_av = aE.n
        h2T_vv = vE.n

        fuse_vl = [0] * 8
        for ig in range(8):
            for il in range(8):
                i = ig * 8 + il
                gslot = i % 3
                WP("gpsimd", wgfree[gslot])
                DMA("gpsimd", wgb[gslot][:, :],
                    wg_d[i * 128:(i + 1) * 128, :], wgsm[gslot])
                WP("gpsimd", wufree[gslot])
                DMA("gpsimd", wub[gslot][:, :],
                    wu8_d[i * 128:(i + 1) * 128, :], wusm[gslot])
                W("tensor", wgsm[gslot])
                W("tensor", wusm[gslot])
                W("tensor", aE, h2T_av)
                W("tensor", vE, h2T_vv)
                PGU = TPB[i % 2]
                WP("tensor", gu_freed[i % 2])
                nmm = 0
                for col0, wring in ((0, wgb[gslot]), (t2, wub[gslot])):
                    for amat in (h2Thi_v, h2Tlo_v):
                        for e_c in range(16):
                            nmm += 1
                            OP("tensor",
                               (lambda e, o=PGU[:, col0:col0 + t2],
                                lh=wring[:, e_c * 128:(e_c + 1) * 128],
                                rh=amat[:, e_c, :],
                                st=(nmm == 1), sp=(nmm == 64):
                                e.matmul(o, lh, rh, start=st, stop=sp)),
                               inc=(tE if nmm == 64 else None))
                gu_v = tE.n
                wgfree[gslot] = (tE, gu_v)
                wufree[gslot] = (tE, gu_v)
                W("scalar", tE, gu_v)
                WP("scalar", gsb_freed)
                OP("scalar", (lambda e, s1=sg_c[:, i:i + 1], p=PGU[:, 0:t2]:
                              e.activation(gsb[:, :], p, AF.Silu,
                                           bias=0.0, scale=s1)), inc=aE)
                OP("scalar", (lambda e, s1=su_c[:, i:i + 1],
                              p=PGU[:, t2:2 * t2]:
                              e.activation(usb[:, :], p,
                                           AF.Copy, bias=0.0, scale=s1)),
                   inc=aE)
                gu_ev = aE.n
                gu_freed[i % 2] = (aE, gu_ev)
                W("vector", aE, gu_ev)
                if il == 0:
                    WP("vector", fuse_freed)
                OP("vector", lambda e: e.tensor_tensor(
                    ffb[:, :], gsb[:, :], usb[:, :], ALU.mult))
                gsb_freed = (vE, vE.n)
                OP("vector", (lambda e, o=fghi[:, il * t2:(il + 1) * t2]:
                              e.tensor_copy(o, ffb[:, :])))
                OP("vector", (lambda e, o=fglo[:, il * t2:(il + 1) * t2],
                              hh=fghi[:, il * t2:(il + 1) * t2]:
                              e.tensor_tensor(o, ffb[:, :], hh, ALU.subtract)),
                   inc=vE)
                fuse_vl[il] = vE.n
            fuse_v = vE.n
            for half in range(2):
                for il in range(8):
                    i = ig * 8 + il
                    dslot = (ig * 16 + half * 8 + il) % 6
                    WP("gpsimd", wdfree[dslot])
                    DMA("gpsimd", wdb[dslot][:, :],
                        wd_d[(i * 2 + half) * 128:(i * 2 + half + 1) * 128, :],
                        wdsm[dslot])
                    W("tensor", wdsm[dslot])
                    W("tensor", vE, fuse_vl[il])
                    if il == 0:
                        WP("tensor", pd_down_freed)
                        if ig == 0 and half == 0:
                            for bkf in wo_bank_freed:
                                WP("tensor", bkf)
                    for sub in range(nsub2):
                        for fmat in (fghi, fglo):
                            for sl2 in range(2):
                                OP("tensor",
                                   (lambda e,
                                    o=PD[:, (sub * 2 + sl2) * 512:
                                         (sub * 2 + sl2 + 1) * 512],
                                    lh=fmat[:, il * t2 + sub * 128:
                                            il * t2 + (sub + 1) * 128],
                                    rh=wdb[dslot][:, sl2 * 512:
                                                  (sl2 + 1) * 512],
                                    st=(il == 0 and fmat is fghi),
                                    sp=(il == 7 and fmat is fglo):
                                    e.matmul(o, lh, rh, start=st, stop=sp)),
                                   inc=(tE if (fmat is fglo and
                                               sub == nsub2 - 1 and
                                               sl2 == 1) else None))
                    wdfree[dslot] = (tE, tE.n)
                dn_v = tE.n
                if half == 1:
                    fuse_freed = (tE, dn_v)
                W("vector", tE, dn_v)
                for sub in range(nsub2):
                    src = PD[:, sub * 1024:(sub + 1) * 1024]
                    dst = accf[:, sub * E_ + half * 1024:
                               sub * E_ + half * 1024 + 1024]
                    if ig == 0:
                        OP("vector", (lambda e, o=dst, i=src:
                                      e.tensor_copy(o, i)),
                           inc=(vE if sub == nsub2 - 1 else None))
                    else:
                        OP("vector", (lambda e, o=dst, i=src:
                                      e.tensor_tensor(o, o, i, ALU.add)),
                           inc=(vE if sub == nsub2 - 1 else None))
                pd_down_freed = (vE, vE.n)
        for sub in range(nsub2):
            acs = accf[:, sub * E_:(sub + 1) * E_]
            xs = xcur[:, sub * E_:(sub + 1) * E_]
            OP("vector", (lambda e, o=acs, a=acs:
                          e.tensor_tensor(o, a, sdr_b[:, :], ALU.mult)))
            OP("vector", (lambda e, o=xs, a=xs, b=acs:
                          e.tensor_tensor(o, a, b, ALU.add)), inc=vE)
            fin_v = vE.n
            W("gpsimd", vE, fin_v)
            DMA("gpsimd", y_d[toff + sub * 128: toff + (sub + 1) * 128, :],
                xs, ystp[xslot])
        x2free[xslot] = (vE, vE.n)
        xst_free[xslot] = (ystp[xslot], ystp[xslot].n)

    W("gpsimd", sem["yst"])
    W("gpsimd", sem["yst1"])

    with nc.Block() as block:

        @block.sync
        def _(e):
            for f in ops["sync"]:
                f(e)

        @block.tensor
        def _(e):
            for f in ops["tensor"]:
                f(e)

        @block.vector
        def _(e):
            for f in ops["vector"]:
                f(e)

        @block.scalar
        def _(e):
            for f in ops["scalar"]:
                f(e)

        @block.gpsimd
        def _(e):
            for f in ops["gpsimd"]:
                f(e)

    p2.close()
    stack.close()
    return nc



def _host_arrays(inputs, tok):
    f32 = np.float32
    Wqk = np.asarray(inputs["L0_attn_Wqk"], f32)
    Wv = np.asarray(inputs["L0_attn_Wv"], f32)
    Wu = np.asarray(inputs["L0_attn_Wu"], f32)
    Wo = np.asarray(inputs["L0_attn_Wo"], f32)
    wg = np.asarray(inputs["L0_ffn_w_gate"])
    wu8 = np.asarray(inputs["L0_ffn_w_up"])
    wd = np.asarray(inputs["L0_ffn_w_down"])

    def attn_w(Wm):
        return np.ascontiguousarray(
            Wm.reshape(16, 128, 16, 128).transpose(2, 1, 0, 3).reshape(E_, H_))

    def colvec(b):
        return np.ascontiguousarray(np.asarray(b, f32).reshape(16, 128).T)

    d = {
        "wqk": attn_w(Wqk), "wv": attn_w(Wv), "wu": attn_w(Wu),
        "wo": np.ascontiguousarray(
            Wo.reshape(16, 128, 4, 512).transpose(2, 0, 1, 3).reshape(-1, 512)),
        "bqk": colvec(inputs["L0_attn_bqk"]),
        "bv": colvec(inputs["L0_attn_bv"]),
        "bu": colvec(inputs["L0_attn_bu"]),
        "bo": np.ascontiguousarray(
            np.asarray(inputs["L0_attn_bo"], f32).reshape(1, E_)),
        "lng": colvec(inputs["L0_attn_ln_g"]),
        "lnb": colvec(inputs["L0_attn_ln_b"]),
        "wg": np.ascontiguousarray(
            wg.reshape(16, 128, 64, 128).transpose(2, 1, 0, 3).reshape(INTER, E_)),
        "wu8": np.ascontiguousarray(
            wu8.reshape(16, 128, 64, 128).transpose(2, 1, 0, 3).reshape(INTER, E_)),
        "wd": np.ascontiguousarray(
            wd.reshape(64, 128, 2, 1024).transpose(0, 2, 1, 3).reshape(-1, 1024)),
        "sg": np.ascontiguousarray(
            np.asarray(inputs["L0_ffn_s_gate"], f32).reshape(64, 128).T),
        "su": np.ascontiguousarray(
            np.asarray(inputs["L0_ffn_s_up"], f32).reshape(64, 128).T),
        "sdr": np.ascontiguousarray(np.broadcast_to(
            np.asarray(inputs["L0_ffn_s_down"], f32).reshape(1, E_),
            (128, E_))),
        "ident": np.eye(128, dtype=f32),
        "onec": np.ones((128, 1), f32),
        "oner": np.ones((1, 128), f32),
    }
    return d


def kernel(**inputs):
    tok = None
    x = np.ascontiguousarray(np.asarray(inputs["L0_attn_x"], np.float32))
    B, N, _ = x.shape
    flat = x.reshape(B * N, E_)
    tok = (B * N) // NCORES
    shared = _host_arrays(inputs, tok)
    nc = build_nc(tok, min(512, tok), 256)
    in_maps = []
    for c in range(NCORES):
        m = dict(shared)
        m["x"] = np.ascontiguousarray(flat[c * tok:(c + 1) * tok])
        in_maps.append(m)
    res = run_bass_kernel_spmd(nc, in_maps, core_ids=list(range(NCORES)))
    out = np.concatenate([res.results[c]["y"] for c in range(NCORES)], axis=0)
    return out.reshape(B, N, E_)


# version 22
# speedup vs baseline: 1.0342x; 1.0342x over previous
import numpy as np
import concourse.bass as bass
import concourse.mybir as mybir
from concourse.bass_utils import run_bass_kernel_spmd
from contextlib import ExitStack

F32 = mybir.dt.float32
F32R = mybir.dt.float32r
BF16 = mybir.dt.bfloat16
I8 = mybir.dt.int8
AF = mybir.ActivationFunctionType
ALU = mybir.AluOpType

E_ = 2048
H_ = 2048
NH = 16
INTER = 8192
EPS = 1e-5
NCORES = 8

ENGS = ("sync", "tensor", "vector", "scalar", "gpsimd")


class Sem:
    def __init__(self, h):
        self.h = h
        self.n = 0

    def inc_(self, k):
        self.n += k
        return self.n


def build_nc(tok, p1_half, t2):
    assert tok % p1_half == 0 and tok % t2 == 0
    assert p1_half in (256, 512) and t2 % 128 == 0 and 256 <= t2 <= 512
    n_half = tok // p1_half
    n_t2 = tok // t2
    nsub1 = p1_half // 128
    nsub2 = t2 // 128
    nw1 = min(512, p1_half)
    nsl1 = p1_half // nw1
    ntb1 = p1_half // 128

    nc = bass.Bass("TRN2", target_bir_lowering=False, debug=False)

    x_d = nc.dram_tensor("x", [tok, E_], F32, kind="ExternalInput").ap()
    wqk_d = nc.dram_tensor("wqk", [E_, H_], F32, kind="ExternalInput").ap()
    wv_d = nc.dram_tensor("wv", [E_, H_], F32, kind="ExternalInput").ap()
    wu_d = nc.dram_tensor("wu", [E_, H_], F32, kind="ExternalInput").ap()
    wo_d = nc.dram_tensor("wo", [4 * 16 * 128, 512], F32, kind="ExternalInput").ap()
    bqk_d = nc.dram_tensor("bqk", [128, 16], F32, kind="ExternalInput").ap()
    bv_d = nc.dram_tensor("bv", [128, 16], F32, kind="ExternalInput").ap()
    bu_d = nc.dram_tensor("bu", [128, 16], F32, kind="ExternalInput").ap()
    bo_d = nc.dram_tensor("bo", [1, E_], F32, kind="ExternalInput").ap()
    lng_d = nc.dram_tensor("lng", [128, 16], F32, kind="ExternalInput").ap()
    lnb_d = nc.dram_tensor("lnb", [128, 16], F32, kind="ExternalInput").ap()
    wg_d = nc.dram_tensor("wg", [INTER, E_], I8, kind="ExternalInput").ap()
    wu8_d = nc.dram_tensor("wu8", [INTER, E_], I8, kind="ExternalInput").ap()
    wd_d = nc.dram_tensor("wd", [INTER * 2, 1024], I8, kind="ExternalInput").ap()
    sg_d = nc.dram_tensor("sg", [128, 64], F32, kind="ExternalInput").ap()
    su_d = nc.dram_tensor("su", [128, 64], F32, kind="ExternalInput").ap()
    sdr_d = nc.dram_tensor("sdr", [128, E_], F32, kind="ExternalInput").ap()
    id_d = nc.dram_tensor("ident", [128, 128], F32, kind="ExternalInput").ap()
    onec_d = nc.dram_tensor("onec", [128, 1], F32, kind="ExternalInput").ap()
    oner_d = nc.dram_tensor("oner", [1, 128], F32, kind="ExternalInput").ap()

    wop_d = nc.dram_tensor("wop", [4 * 16 * 128, 1024], BF16).ap()
    qsp_d = nc.dram_tensor("qsp", [H_, tok], F32).ap()
    usp_d = nc.dram_tensor("usp", [H_, tok], F32).ap()
    kvin_d = nc.dram_tensor("kvin", [128, NH * 128], F32)
    kvout_d = nc.dram_tensor("kvout", [128, NH * 128], F32)

    y_d = nc.dram_tensor("y", [tok, E_], F32, kind="ExternalOutput").ap()

    ops = {e: [] for e in ENGS}

    def E(eng, fn):
        ops[eng].append(fn)

    def W(eng, s, val=None):
        v = s.n if val is None else val
        if v > 0:
            E(eng, lambda e, h=s.h, v=v: e.wait_ge(h, v))

    def WP(eng, pair):
        if pair is not None:
            W(eng, pair[0], pair[1])

    def OP(eng, build, inc=None, k=1, dma=False):
        if not dma and eng == "vector":
            assert inc is None or inc is vE
            W(eng, vE, vE.n)
            inc = vE
        elif not dma and eng == "scalar":
            assert inc is None or inc is aE
            W(eng, aE, aE.n)
            inc = aE
        if inc is not None:
            inc.inc_(k)
            E(eng, lambda e, b=build, h=inc.h, k=k: b(e).then_inc(h, k))
        else:
            E(eng, lambda e, b=build: b(e))

    def DMA(eng, out, in_, s):
        OP(eng, (lambda e, o=out, i=in_: e.dma_start(out=o, in_=i)), inc=s,
           k=16, dma=True)

    stack = ExitStack()
    semnames = ("tE vE aE const x0 x1 wf0 wf1 wh0 wh1 spq spu kvb cc "
                "xq0 xq1 uq0 uq1 wof0 wof1 woh0 woh1 wgs0 wgs1 wus0 wus1 "
                "wds0 wds1 wds2 wds3 wds4 wds5 xq2 xq3 uq2 uq3 x2c "
                "yst x20 x21 spw0 spw1 gE "
                "wol0 wol1 wol2 wgs2 wus2 wohx2 wohx3 wohx4 wohx5 "
                "wolx3 wolx4 wolx5 wpa0 wpa1 wpb0 wpb1 yst1 x2b").split()
    sem = {n: Sem(stack.enter_context(nc.semaphore(n))) for n in semnames}
    tE, vE, aE = sem["tE"], sem["vE"], sem["aE"]

    PD = stack.enter_context(nc.psum_tensor("PD", [128, 2048], F32))
    PG = stack.enter_context(nc.psum_tensor("PG", [128, 512], F32))
    PU = stack.enter_context(nc.psum_tensor("PU", [128, 512], F32))
    PX = stack.enter_context(nc.psum_tensor("PX", [128, 512], F32))
    PY = stack.enter_context(nc.psum_tensor("PY", [128, 512], F32))
    TPB = (PG, PU)

    def sb(name, shape, dt, st=None):
        return (st or stack).enter_context(nc.sbuf_tensor(name, shape, dt))

    ident = sb("identb", [128, 128], F32)
    onec = sb("onecb", [128, 1], F32)
    oner = sb("onerb", [1, 128], F32)
    bqk_c = sb("bqkc", [128, 16], F32)
    bv_c = sb("bvc", [128, 16], F32)
    bu_c = sb("buc", [128, 16], F32)
    lng_c = sb("lngc", [128, 16], F32)
    lnb_c = sb("lnbc", [128, 16], F32)
    bo_r = sb("bor", [1, E_], F32)
    sg_c = sb("sgc", [128, 64], F32)
    su_c = sb("suc", [128, 64], F32)
    sdr_b = sb("sdrb", [128, E_], F32)
    kv_sb = sb("kvsb", [128, NH * 128], F32)

    sc = sem["const"]
    for dst, src in ((ident, id_d), (onec, onec_d), (oner, oner_d),
                     (bqk_c, bqk_d), (bv_c, bv_d), (bu_c, bu_d),
                     (lng_c, lng_d), (lnb_c, lnb_d), (bo_r, bo_d),
                     (sg_c, sg_d), (su_c, su_d), (sdr_b, sdr_d)):
        DMA("sync", dst[:, :], src, sc)
    for eng in ("tensor", "vector", "scalar"):
        W(eng, sc)

    p1 = ExitStack()
    n_h1buf = 2
    h1T_hi = [sb("h1Thi%d" % b, [128, 16 * p1_half], BF16, p1)
              for b in range(n_h1buf)]
    h1T_lo = [sb("h1Tlo%d" % b, [128, 16 * p1_half], BF16, p1)
              for b in range(n_h1buf)]
    h1Thi_vb = [t[:, :].rearrange("p (e t) -> p e t", e=16) for t in h1T_hi]
    h1Tlo_vb = [t[:, :].rearrange("p (e t) -> p e t", e=16) for t in h1T_lo]
    xb1 = [sb("xb1%d" % i, [128, E_], F32, p1) for i in range(3)]
    h1b = [sb("h1b%d" % i, [128, E_], F32, p1) for i in range(3)]
    j1 = sb("j1", [128, E_], F32, p1)
    j2 = sb("j2", [128, E_], F32, p1)
    statc = sb("statc", [128, 8], F32, p1)
    wfb = [sb("wf0b", [128, E_], F32, p1), sb("wf1b", [128, E_], F32, p1)]
    whb = [sb("wh0b", [128, E_], BF16, p1), sb("wh1b", [128, E_], BF16, p1)]
    wlb = [sb("wl0b", [128, E_], BF16, p1), sb("wl1b", [128, E_], BF16, p1)]
    wpf = [sb("wpf0", [128, 512], F32, p1)]
    wph = [sb("wph0", [128, 512], BF16, p1)]
    wpl = [sb("wpl0", [128, 512], BF16, p1)]
    qfm = sb("qfm", [128, p1_half], F32, p1)
    vfm = sb("vfm", [128, p1_half], F32, p1)
    ufm = sb("ufm", [128, p1_half], F32, p1)
    qtm = sb("qtm", [128, p1_half], F32, p1)
    vtm = sb("vtm", [128, p1_half], F32, p1)
    tmpa = sb("tmpa", [128, p1_half], F32, p1)
    tmpb = sb("tmpb", [128, p1_half], F32, p1)

    xsem = [sem["x0"], sem["x1"], sem["x2b"]]
    xfree = [None, None, None]
    wfsem = [sem["wf0"], sem["wf1"]]
    whsem = [sem["wh0"], sem["wh1"]]
    wfree = [None, None]
    wo_j = [0]

    wpfree = [None, None]

    def wo_presplit(n_slabs):
        for _ in range(n_slabs):
            j = wo_j[0]
            if j >= 64:
                return
            wo_j[0] += 1
            slot = 0
            if wpfree[slot] is not None:
                W("sync", *wpfree[slot])
                W("gpsimd", *wpfree[slot])
            srcw = wo_d[j * 128:(j + 1) * 128, :]
            wpa = sem["wpa0"]
            wpb = sem["wpb0"]
            DMA("sync", wpf[slot][:, :], srcw, wpa)
            DMA("gpsimd", wph[slot][:, :], srcw, wpb)
            W("gpsimd", wpa)
            W("gpsimd", wpb)
            OP("gpsimd", (lambda e, o=wpl[slot][:, :],
                          a=wpf[slot][:, :],
                          b=wph[slot][:, :]:
                          e.tensor_tensor(o, a, b, ALU.subtract)),
               inc=sem["gE"])
            W("gpsimd", sem["gE"], sem["gE"].n)
            spw = sem["spw0"]
            DMA("gpsimd", wop_d[j * 128:(j + 1) * 128, 0:512],
                wph[slot][:, :], spw)
            DMA("gpsimd", wop_d[j * 128:(j + 1) * 128, 512:1024],
                wpl[slot][:, :], spw)
            wpfree[slot] = (spw, spw.n)
    tp_freed = [None, None]
    tp_ctr = [0]
    bank_freed = [None, None, None, None]
    px_freed = [None]
    kv_seeded = [False] * NH
    h1_freed = [None, None, None]

    h1T_freed = [None, None]
    h1T_ready = [0, 0]

    h1_ready = {}

    def a_step_stats(half, s):
        off = half * p1_half
        slot = (half * nsub1 + s) % 3
        if xfree[slot] is not None:
            W("sync", *xfree[slot])
        DMA("sync", xb1[slot][:, :],
            x_d[off + s * 128: off + (s + 1) * 128, :], xsem[slot])
        xv = xsem[slot].n
        W("vector", xsem[slot], xv)
        OP("vector", (lambda e, i=xb1[slot][:, :]:
                      e.tensor_scalar(j1[:, :], i, 1.0, 0.0, ALU.mult,
                                      ALU.add,
                                      accum_out=statc[:, 0:1])), inc=vE)
        W("vector", vE)
        W("scalar", xsem[slot], xv)
        OP("scalar", (lambda e, i=xb1[slot][:, :]:
                      e.activation(j2[:, :], i, AF.Square,
                                   accum_out=statc[:, 1:2])), inc=aE)
        W("vector", aE)
        OP("vector", lambda e: e.tensor_scalar(
            statc[:, 2:3], statc[:, 0:1], 1.0 / E_, None, ALU.mult))
        OP("vector", lambda e: e.tensor_tensor(
            statc[:, 3:4], statc[:, 2:3], statc[:, 2:3], ALU.mult))
        OP("vector", lambda e: e.tensor_scalar(
            statc[:, 4:5], statc[:, 1:2], 1.0 / E_, None, ALU.mult))
        OP("vector", lambda e: e.tensor_tensor(
            statc[:, 3:4], statc[:, 4:5], statc[:, 3:4], ALU.subtract))
        OP("vector", lambda e: e.tensor_scalar(
            statc[:, 3:4], statc[:, 3:4], EPS, None, ALU.add))
        OP("vector", lambda e: e.reciprocal(
            statc[:, 4:5], statc[:, 3:4]), inc=vE)
        W("scalar", vE)
        OP("scalar", lambda e: e.activation(
            statc[:, 4:5], statc[:, 4:5], AF.Sqrt), inc=aE)
        hb = h1b[slot]
        WP("vector", h1_freed[slot])
        W("vector", aE)
        OP("vector", (lambda e, i=xb1[slot][:, :], o=hb[:, :]:
                      e.tensor_scalar(o, i, statc[:, 2:3], statc[:, 4:5],
                                      ALU.subtract, ALU.mult)), inc=vE)
        h1_v = vE.n
        xfree[slot] = (vE, h1_v)
        h1_ready[(half, s)] = h1_v

    def a_step_tp(half, s):
        buf = half % n_h1buf
        slot = (half * nsub1 + s) % 3
        hb = h1b[slot]
        h1_v = h1_ready[(half, s)]
        for eg in range(4):
            tpi = tp_ctr[0] % 2
            tp = TPB[tpi]
            W("tensor", vE, h1_v)
            WP("tensor", tp_freed[tpi])
            for k in range(4):
                OP("tensor",
                   (lambda e, o=tp[:, k * 128:(k + 1) * 128],
                    d=hb[:, (eg * 4 + k) * 128:(eg * 4 + k + 1) * 128],
                    st=(k == 0), sp=(k == 3):
                    e.matmul(o, d, ident[:, :], start=st, stop=sp,
                             is_transpose=True)),
                   inc=(tE if k == 3 else None))
            tv = tE.n
            dhi = h1Thi_vb[buf][:, eg * 4:(eg + 1) * 4, s * 128:(s + 1) * 128]
            dlo = h1Tlo_vb[buf][:, eg * 4:(eg + 1) * 4, s * 128:(s + 1) * 128]
            srcv = tp[:, :512].rearrange("p (b c) -> p b c", b=4)
            W("scalar", tE, tv)
            WP("scalar", h1T_freed[buf])
            OP("scalar", (lambda e, o=dhi, i=srcv:
                          e.activation(o, i, AF.Copy)), inc=aE)
            av = aE.n
            W("vector", aE, av)
            OP("vector", (lambda e, o=dlo, i=srcv, h2=dhi:
                          e.tensor_tensor(o, i, h2, ALU.subtract)), inc=vE)
            tp_freed[tpi] = (vE, vE.n)
            tp_ctr[0] += 1
        h1_freed[slot] = (vE, vE.n)
        if s == nsub1 - 1:
            h1T_ready[buf] = vE.n
        if half < 2:
            wo_presplit((64 // (min(2, n_half) * nsub1)) +
                        (64 % (min(2, n_half) * nsub1) if
                         (half == min(2, n_half) - 1 and s == nsub1 - 1)
                         else 0))

    def a_step_sub(half, s):
        a_step_stats(half, s)
        a_step_tp(half, s)

    def w_loop(half):
        buf = half % n_h1buf
        off = half * p1_half
        h1Thi_v = h1Thi_vb[buf]
        h1Tlo_v = h1Tlo_vb[buf]
        projs = (("q", wqk_d, bqk_c), ("v", wv_d, bv_c), ("u", wu_d, bu_c))

        def w_load(gl):
            m2, pi2 = divmod(gl, 3)
            slot = (half * 48 + gl) % 2
            wsrc = projs[pi2][1]
            if wfree[slot] is not None:
                W("sync", *wfree[slot])
                W("gpsimd", *wfree[slot])
            DMA("sync", wfb[slot][:, :],
                wsrc[m2 * 128:(m2 + 1) * 128, :], wfsem[slot])
            DMA("gpsimd", whb[slot][:, :],
                wsrc[m2 * 128:(m2 + 1) * 128, :], whsem[slot])

        def w_sub(gl):
            slot = (half * 48 + gl) % 2
            W("vector", wfsem[slot])
            W("vector", whsem[slot])
            OP("vector", (lambda e, o=wlb[slot][:, :], a=wfb[slot][:, :],
                          b=whb[slot][:, :]:
                          e.tensor_tensor(o, a, b, ALU.subtract)), inc=vE)
            sub_v[slot] = vE.n

        sub_v = [0, 0]
        w_load(0)
        w_load(1)
        w_sub(0)
        eps = {}
        hooks = {}
        if n_h1buf == 2 and half + 1 < n_half:
            for s2 in range(nsub1):
                hooks[min(12 + s2 * max(1, 32 // nsub1), 40)] = \
                    ("full", half + 1, s2)
        elif half + 1 < n_half:
            hooks[40] = ("stats", half + 1, 0)
            hooks[44] = ("stats", half + 1, 1)
        for gl in range(48):
            if gl in hooks:
                kind2, h2_, s2_ = hooks[gl]
                if kind2 == "full":
                    a_step_sub(h2_, s2_)
                else:
                    a_step_stats(h2_, s2_)
            m, pi = divmod(gl, 3)
            kind, wsrc, bias = projs[pi]
            g = half * 48 + gl
            slot = g % 2
            bank = g % 4
            pp = PD[:, bank * 512: bank * 512 + 512]
            if gl + 1 < 48:
                w_sub(gl + 1)
            W("tensor", vE, max(sub_v[slot], h1T_ready[buf]))
            WP("tensor", bank_freed[bank])
            first = True
            for p_i, (arhs, wmat) in enumerate(
                    ((h1Thi_v, whb[slot]), (h1Tlo_v, whb[slot]),
                     (h1Thi_v, wlb[slot]))):
                for e_c in range(16):
                    for ns in range(nsl1):
                        last = (p_i == 2 and e_c == 15 and ns == nsl1 - 1)
                        OP("tensor",
                           (lambda e, o=pp[:, ns * nw1:(ns + 1) * nw1],
                            lh=wmat[:, e_c * 128:(e_c + 1) * 128],
                            rh=arhs[:, e_c, ns * nw1:(ns + 1) * nw1],
                            st=first, sp=last:
                            e.matmul(o, lh, rh, start=st, stop=sp)),
                           inc=(tE if last else None))
                        first = False
            mm_v = tE.n
            wfree[slot] = (tE, mm_v)
            if gl + 2 < 48:
                w_load(gl + 2)
            if kind == "q":
                W("vector", tE, mm_v)
                W("vector", sem["spq"], sem["spq"].n)
                OP("vector", (lambda e, b=bias[:, m:m + 1], p=pp:
                              e.tensor_scalar(tmpa[:, :], p[:, :p1_half],
                                              b, None, ALU.add)))
                OP("vector", lambda e: e.tensor_scalar(
                    tmpb[:, :], tmpa[:, :], 0.0, None, ALU.min), inc=vE)
                W("scalar", vE)
                OP("scalar", lambda e: e.activation(
                    tmpb[:, :], tmpb[:, :], AF.Exp), inc=aE)
                OP("vector", lambda e: e.tensor_scalar(
                    qfm[:, :], tmpa[:, :], 0.0, -1.0, ALU.max, ALU.add))
                W("vector", aE)
                OP("vector", lambda e: e.tensor_tensor(
                    qfm[:, :], qfm[:, :], tmpb[:, :], ALU.add), inc=vE)
                ep = (vE, vE.n)
                bank_freed[bank] = ep
                W("gpsimd", vE, ep[1])
                DMA("gpsimd",
                    qsp_d[m * 128:(m + 1) * 128, off:off + p1_half],
                    qfm[:, :], sem["spq"])
                eps["q"] = ep
            elif kind == "v":
                W("scalar", tE, mm_v)
                OP("scalar", (lambda e, b=bias[:, m:m + 1], p=pp:
                              e.activation(vfm[:, :], p[:, :p1_half],
                                           AF.Silu, bias=b, scale=1.0)),
                   inc=aE)
                ep = (aE, aE.n)
                bank_freed[bank] = ep
                eps["v"] = ep
            else:
                W("scalar", tE, mm_v)
                W("scalar", sem["spu"], sem["spu"].n)
                OP("scalar", (lambda e, b=bias[:, m:m + 1], p=pp:
                              e.activation(ufm[:, :], p[:, :p1_half],
                                           AF.Silu, bias=b, scale=1.0)),
                   inc=aE)
                ep = (aE, aE.n)
                bank_freed[bank] = ep
                W("gpsimd", aE, ep[1])
                DMA("gpsimd",
                    usp_d[m * 128:(m + 1) * 128, off:off + p1_half],
                    ufm[:, :], sem["spu"])
                for src_fm, dst_tm, epk in ((qfm, qtm, "q"), (vfm, vtm, "v")):
                    for tb0 in range(0, ntb1, 4):
                        nblk = min(4, ntb1 - tb0)
                        tpi = tp_ctr[0] % 2
                        tp = TPB[tpi]
                        WP("tensor", eps[epk])
                        WP("tensor", tp_freed[tpi])
                        for k in range(nblk):
                            OP("tensor",
                               (lambda e, o=tp[:, k * 128:(k + 1) * 128],
                                d=src_fm[:, (tb0 + k) * 128:
                                         (tb0 + k + 1) * 128],
                                st=(k == 0), sp=(k == nblk - 1):
                                e.matmul(o, d, ident[:, :], start=st, stop=sp,
                                         is_transpose=True)),
                               inc=(tE if k == nblk - 1 else None))
                        tv = tE.n
                        W("vector", tE, tv)
                        OP("vector",
                           (lambda e, o=dst_tm[:, tb0 * 128:
                                               tb0 * 128 + nblk * 128],
                            i=tp[:, :nblk * 128]:
                            e.tensor_copy(o, i)), inc=vE)
                        tp_freed[tpi] = (vE, vE.n)
                kv_in_v = vE.n
                W("tensor", vE, kv_in_v)
                WP("tensor", px_freed[0])
                for b in range(ntb1):
                    OP("tensor",
                       (lambda e, lh=qtm[:, b * 128:(b + 1) * 128],
                        rh=vtm[:, b * 128:(b + 1) * 128],
                        st=(b == 0), sp=(b == ntb1 - 1):
                        e.matmul(PX[:, :128], lh, rh, start=st, stop=sp)),
                       inc=(tE if b == ntb1 - 1 else None))
                kvmm_v = tE.n
                W("vector", tE, kvmm_v)
                if not kv_seeded[m]:
                    OP("vector", (lambda e, o=kv_sb[:, m * 128:(m + 1) * 128]:
                                  e.tensor_copy(o, PX[:, :128])), inc=vE)
                    kv_seeded[m] = True
                else:
                    OP("vector", (lambda e, o=kv_sb[:, m * 128:(m + 1) * 128]:
                                  e.tensor_tensor(o, o, PX[:, :128], ALU.add)),
                       inc=vE)
                px_freed[0] = (vE, vE.n)
        h1T_freed[buf] = (tE, tE.n)

    for s0 in range(nsub1):
        a_step_sub(0, s0)
    for half in range(n_half):
        w_loop(half)
        if n_h1buf == 1 and half + 1 < n_half:
            for s0 in range(nsub1):
                if (half + 1, s0) in h1_ready:
                    a_step_tp(half + 1, s0)
                else:
                    a_step_sub(half + 1, s0)

    spq_done = sem["spq"].n
    spu_done = sem["spu"].n
    spw_done = (sem["spw0"].n, sem["spw1"].n)
    assert wo_j[0] == 64
    W("gpsimd", vE)
    DMA("gpsimd", kvin_d.ap(), kv_sb[:, :], sem["kvb"])
    W("gpsimd", sem["kvb"])
    OP("gpsimd", lambda e: e.collective_compute(
        "AllReduce", ALU.add, replica_groups=[[0, 1, 2, 3], [4, 5, 6, 7]],
        ins=[kvin_d.ap()], outs=[kvout_d.ap()]), inc=sem["cc"])
    W("gpsimd", sem["cc"])
    DMA("gpsimd", kv_sb[:, :], kvout_d.ap(), sem["kvb"])
    kv_final = sem["kvb"].n

    p1t, p1v, p1a = tE.n, vE.n, aE.n
    for eng in ENGS:
        W(eng, tE, p1t)
        W(eng, vE, p1v)
        W(eng, aE, p1a)
        W(eng, sem["spq"], spq_done)
        W(eng, sem["spu"], spu_done)
        W(eng, sem["spw0"], spw_done[0])
        W(eng, sem["spw1"], spw_done[1])
        W(eng, sem["kvb"], kv_final)

    p1.close()

    p2 = ExitStack()
    xb2 = [sb("xb2a", [128, nsub2 * E_], F32, p2),
           sb("xb2b", [128, nsub2 * E_], F32, p2)]
    qhb = [sb("qh%d" % i, [128, t2], F32, p2) for i in range(4)]
    uhb = [sb("uh%d" % i, [128, t2], F32, p2) for i in range(4)]
    outsb = sb("outsb", [128, NH * t2], F32, p2)
    sqb = sb("sqb", [128, t2], F32, p2)
    rowb = sb("rowb", [1, 5 * t2], F32, p2)
    afb2 = [sb("afb%d" % i, [128, t2], F32, p2) for i in range(2)]
    af22 = [sqb, sb("af2b", [128, t2], F32, p2)]
    arena_hi = sb("arenahi", [128, NH * t2], BF16, p2)
    arena_lo = sb("arenalo", [128, NH * t2], BF16, p2)
    h2T_hi_t = sb("h2thi", [128, NH * t2], BF16, p2)
    h2T_lo_t = sb("h2tlo", [128, NH * t2], BF16, p2)
    h2Thi_v = h2T_hi_t[:, :].rearrange("p (e t) -> p e t", e=16)
    h2Tlo_v = h2T_lo_t[:, :].rearrange("p (e t) -> p e t", e=16)
    wopb = [sb("wop%d" % i, [128, 1024], BF16, p2) for i in range(6)]
    h2b = sb("h2b", [128, E_], F32, p2)
    st2 = sb("st2", [128, 8], F32, p2)
    wgb = [sb("wg%db" % i, [128, E_], BF16, p2) for i in range(3)]
    wub = [sb("wu%db" % i, [128, E_], BF16, p2) for i in range(3)]
    wdb = [sb("wd%db" % i, [128, 1024], BF16, p2) for i in range(6)]
    gsb = sb("gsb", [128, t2], F32, p2)
    usb = sb("usb", [128, t2], F32, p2)
    ffb = sb("ffb", [128, t2], F32, p2)
    fghi = sb("fghi", [128, 8 * t2], BF16, p2)
    fglo = sb("fglo", [128, 8 * t2], BF16, p2)
    assert NH * t2 == nsub2 * E_
    accf = outsb

    x2sem = [sem["x20"], sem["x21"]]
    qsem = [sem["xq0"], sem["xq1"], sem["xq2"], sem["xq3"]]
    usem = [sem["uq0"], sem["uq1"], sem["uq2"], sem["uq3"]]
    wos4 = [sem["wof0"], sem["wof1"], sem["woh0"], sem["woh1"],
            sem["wohx2"], sem["wohx3"]]
    wgsm = [sem["wgs0"], sem["wgs1"], sem["wgs2"]]
    wusm = [sem["wus0"], sem["wus1"], sem["wus2"]]
    wdsm = [sem["wds0"], sem["wds1"], sem["wds2"], sem["wds3"],
            sem["wds4"], sem["wds5"]]

    qfree = [None] * 4
    ufree = [None] * 4
    wofree = [None] * 6
    wgfree = [None, None, None]
    wufree = [None, None, None]
    wdfree = [None] * 6
    x2free = [None, None]
    xst_free = [None, None]
    ystp = [sem["yst"], sem["yst1"]]
    oev_freed = [None, None]
    wo_bank_freed = [None] * 4
    pd_down_freed = None
    fuse_freed = None
    gu_freed = [None, None]
    h2tp_freed = [None, None]
    stats_freed = None
    h2_freed = None


    qctr = 0
    woctr = 0
    tpctr2 = 0
    gsb_freed = None
    ffb_freed = None

    for t in range(n_t2):
        toff = t * t2
        xslot = t % 2
        xcur = xb2[xslot]
        WP("sync", x2free[xslot])
        WP("sync", xst_free[xslot])
        for s in range(nsub2):
            DMA("sync", xcur[:, s * E_:(s + 1) * E_],
                x_d[toff + s * 128: toff + (s + 1) * 128, :], x2sem[xslot])
        xv2 = x2sem[xslot].n

        WP("tensor", stats_freed)
        WP("tensor", gu_freed[0])
        WP("tensor", gu_freed[1])
        WP("tensor", h2tp_freed[0])
        WP("tensor", h2tp_freed[1])
        W("scalar", vE, vE.n)
        for h in range(NH):
            slot = qctr % 4
            WP("sync", qfree[slot])
            DMA("sync", qhb[slot][:, :],
                qsp_d[h * 128:(h + 1) * 128, toff:toff + t2], qsem[slot])
            qv = qsem[slot].n
            tpi = h % 2
            tp = TPB[tpi]
            W("tensor", qsem[slot], qv)
            WP("tensor", oev_freed[tpi])
            OP("tensor", (lambda e, lh=kv_sb[:, h * 128:(h + 1) * 128],
                          rh=qhb[slot][:, :], o=tp[:, :t2]:
                          e.matmul(o, lh, rh, start=True, stop=True)), inc=tE)
            ov = tE.n
            qfree[slot] = (tE, ov)
            W("scalar", tE, ov)
            OP("scalar", (lambda e, o=outsb[:, h * t2:(h + 1) * t2],
                          i=tp[:, :t2]: e.activation(o, i, AF.Copy)), inc=aE)
            OP("scalar", (lambda e, i=tp[:, :t2]:
                          e.activation(sqb[:, :], i, AF.Square)), inc=aE)
            sq_v = aE.n
            oev_freed[tpi] = (aE, sq_v)
            qctr += 1
            W("tensor", aE, sq_v)
            OP("tensor", (lambda e, rh=outsb[:, h * t2:(h + 1) * t2],
                          st=(h == 0), sp=(h == NH - 1):
                          e.matmul(PX[:1, :t2], onec[:, :], rh,
                                   start=st, stop=sp)),
               inc=(tE if h == NH - 1 else None))
            OP("tensor", (lambda e, st=(h == 0), sp=(h == NH - 1):
                          e.matmul(PY[:1, :t2], onec[:, :], sqb[:, :],
                                   start=st, stop=sp)),
               inc=(tE if h == NH - 1 else None))
        stats_v = tE.n
        R = rowb
        W("vector", tE, stats_v)
        OP("vector", lambda e: e.tensor_scalar(
            R[:1, 0 * t2:1 * t2], PX[:1, :t2], 1.0 / H_, None, ALU.mult))
        OP("vector", lambda e: e.tensor_scalar(
            R[:1, 1 * t2:2 * t2], PY[:1, :t2], 1.0 / H_, None, ALU.mult))
        OP("vector", lambda e: e.tensor_tensor(
            R[:1, 2 * t2:3 * t2], R[:1, 0 * t2:1 * t2], R[:1, 0 * t2:1 * t2],
            ALU.mult))
        OP("vector", lambda e: e.tensor_tensor(
            R[:1, 2 * t2:3 * t2], R[:1, 1 * t2:2 * t2], R[:1, 2 * t2:3 * t2],
            ALU.subtract))
        OP("vector", lambda e: e.tensor_scalar(
            R[:1, 2 * t2:3 * t2], R[:1, 2 * t2:3 * t2], EPS, None, ALU.add))
        OP("vector", lambda e: e.reciprocal(
            R[:1, 3 * t2:4 * t2], R[:1, 2 * t2:3 * t2]), inc=vE)
        W("scalar", vE)
        OP("scalar", lambda e: e.activation(
            R[:1, 3 * t2:4 * t2], R[:1, 3 * t2:4 * t2], AF.Sqrt), inc=aE)
        W("vector", aE)
        OP("vector", lambda e: e.tensor_tensor(
            R[:1, 4 * t2:5 * t2], R[:1, 0 * t2:1 * t2], R[:1, 3 * t2:4 * t2],
            ALU.mult), inc=vE)
        rows_v = vE.n
        W("tensor", vE, rows_v)
        OP("tensor", lambda e: e.matmul(
            PX[:, :t2], oner[:, :], R[:1, 3 * t2:4 * t2],
            start=True, stop=True), inc=tE)
        OP("tensor", lambda e: e.matmul(
            PY[:, :t2], oner[:, :], R[:1, 4 * t2:5 * t2],
            start=True, stop=True), inc=tE)
        bc_v = tE.n

        a_vh = [0] * NH
        afb_freed = [None, None]
        for h in range(NH):
            slot = h % 4
            WP("scalar", ufree[slot])
            DMA("scalar", uhb[slot][:, :],
                usp_d[h * 128:(h + 1) * 128, toff:toff + t2], usem[slot])
            uv = usem[slot].n
            afb = afb2[h % 2]
            af2 = af22[h % 2]
            W("vector", tE, bc_v)
            W("vector", usem[slot], uv)
            WP("vector", afb_freed[h % 2])
            OP("vector", (lambda e, i=outsb[:, h * t2:(h + 1) * t2], o=afb:
                          e.tensor_tensor(o[:, :], i, PX[:, :t2], ALU.mult)))
            OP("vector", (lambda e, o=afb: e.tensor_tensor(
                o[:, :], o[:, :], PY[:, :t2], ALU.subtract)), inc=vE)
            W("scalar", vE, vE.n)
            OP("scalar", (lambda e, gg=lng_c[:, h:h + 1], bb=lnb_c[:, h:h + 1],
                          o=af2, i=afb:
                          e.activation(o[:, :], i[:, :], AF.Identity,
                                       bias=bb, scale=gg)), inc=aE)
            W("vector", aE, aE.n)
            OP("vector", (lambda e, u=uhb[slot][:, :], o=af2:
                          e.tensor_tensor(o[:, :], o[:, :], u, ALU.mult)),
               inc=vE)
            ufree[slot] = (vE, vE.n)
            W("scalar", vE, vE.n)
            OP("scalar", (lambda e, o=arena_hi[:, h * t2:(h + 1) * t2], i=af2:
                          e.activation(o, i[:, :], AF.Copy)), inc=aE)
            afb_hi_a = aE.n
            W("vector", aE, aE.n)
            OP("vector", (lambda e, o=arena_lo[:, h * t2:(h + 1) * t2],
                          hh=arena_hi[:, h * t2:(h + 1) * t2],
                          i=af2:
                          e.tensor_tensor(o, i[:, :], hh, ALU.subtract)),
               inc=vE)
            a_vh[h] = vE.n
            afb_freed[h % 2] = (vE, vE.n)
        a_v = vE.n
        stats_freed = (vE, a_v)

        for p in range(2):
            for h in range(NH):
                for sl in (2 * p, 2 * p + 1):
                    slot = woctr % 6
                    WP("sync", wofree[slot])
                    DMA("sync", wopb[slot][:, :],
                        wop_d[(sl * 16 + h) * 128:(sl * 16 + h + 1) * 128, :],
                        wos4[slot])
                    W("tensor", wos4[slot])
                    W("tensor", vE, a_vh[h])
                    if h == 0:
                        WP("tensor", pd_down_freed)
                        WP("tensor", wo_bank_freed[(sl % 2) * 2])
                        WP("tensor", wo_bank_freed[(sl % 2) * 2 + 1])
                    for sub in range(nsub2):
                        bank = (sl % 2) * nsub2 + sub
                        pp = PD[:, bank * 512:(bank + 1) * 512]
                        if h == 0:
                            OP("tensor", (lambda e, o=pp,
                                          rh=bo_r[:1, sl * 512:(sl + 1) * 512]:
                                          e.matmul(o, oner[:, :], rh,
                                                   start=True, stop=False)))
                        for pi2, (amat, wmat) in enumerate(
                                ((arena_hi, wopb[slot][:, 0:512]),
                                 (arena_lo, wopb[slot][:, 0:512]),
                                 (arena_hi, wopb[slot][:, 512:1024]))):
                            sp = (h == NH - 1 and pi2 == 2)
                            OP("tensor",
                               (lambda e, o=pp,
                                lh=amat[:, h * t2 + sub * 128:
                                        h * t2 + (sub + 1) * 128],
                                rh=wmat[:, :], sp=sp:
                                e.matmul(o, lh, rh, start=False, stop=sp)),
                               inc=(tE if (pi2 == 2 and
                                           sub == nsub2 - 1) else None))
                    wofree[slot] = (tE, tE.n)
                    woctr += 1
            grp_v = tE.n
            W("vector", tE, grp_v)
            W("vector", x2sem[xslot], xv2)
            for sl in (2 * p, 2 * p + 1):
                for sub in range(nsub2):
                    bank = (sl % 2) * nsub2 + sub
                    pp = PD[:, bank * 512:(bank + 1) * 512]
                    xs = xcur[:, sub * E_ + sl * 512:
                              sub * E_ + (sl + 1) * 512]
                    OP("vector", (lambda e, o=xs, a=xs, b=pp:
                                  e.tensor_tensor(o, a, b, ALU.add)), inc=vE)
                    wo_bank_freed[bank] = (vE, vE.n)
        xnew_v = vE.n
        x2free[xslot] = (vE, xnew_v)

        for s in range(nsub2):
            xs = xcur[:, s * E_:(s + 1) * E_]
            W("vector", vE, xnew_v)
            OP("vector", (lambda e, i=xs:
                          e.tensor_scalar(accf[:, 0:E_], i, 1.0, 0.0,
                                          ALU.mult, ALU.add,
                                          accum_out=st2[:, 0:1])), inc=vE)
            W("vector", vE)
            W("scalar", vE, xnew_v)
            OP("scalar", (lambda e, i=xs:
                          e.activation(accf[:, E_:2 * E_], i, AF.Square,
                                       accum_out=st2[:, 1:2])), inc=aE)
            W("vector", aE)
            OP("vector", lambda e: e.tensor_scalar(
                st2[:, 2:3], st2[:, 0:1], 1.0 / E_, None, ALU.mult))
            OP("vector", lambda e: e.tensor_tensor(
                st2[:, 3:4], st2[:, 2:3], st2[:, 2:3], ALU.mult))
            OP("vector", lambda e: e.tensor_scalar(
                st2[:, 4:5], st2[:, 1:2], 1.0 / E_, None, ALU.mult))
            OP("vector", lambda e: e.tensor_tensor(
                st2[:, 3:4], st2[:, 4:5], st2[:, 3:4], ALU.subtract))
            OP("vector", lambda e: e.tensor_scalar(
                st2[:, 3:4], st2[:, 3:4], EPS, None, ALU.add))
            OP("vector", lambda e: e.reciprocal(
                st2[:, 4:5], st2[:, 3:4]), inc=vE)
            W("scalar", vE)
            OP("scalar", lambda e: e.activation(
                st2[:, 4:5], st2[:, 4:5], AF.Sqrt), inc=aE)
            WP("vector", h2_freed)
            W("vector", aE)
            OP("vector", (lambda e, i=xs:
                          e.tensor_scalar(h2b[:, :], i, st2[:, 2:3],
                                          st2[:, 4:5], ALU.subtract,
                                          ALU.mult)), inc=vE)
            h2_v = vE.n
            for eg in range(4):
                tpi = tpctr2 % 2
                tp = TPB[tpi]
                W("tensor", vE, h2_v)
                WP("tensor", h2tp_freed[tpi])
                WP("tensor", oev_freed[tpi])
                oev_freed[tpi] = None
                WP("tensor", gu_freed[tpi])
                gu_freed[tpi] = None
                for k in range(4):
                    OP("tensor",
                       (lambda e, o=tp[:, k * 128:(k + 1) * 128],
                        d=h2b[:, (eg * 4 + k) * 128:(eg * 4 + k + 1) * 128],
                        st=(k == 0), sp=(k == 3):
                        e.matmul(o, d, ident[:, :], start=st, stop=sp,
                                 is_transpose=True)),
                       inc=(tE if k == 3 else None))
                tv = tE.n
                dhi = h2Thi_v[:, eg * 4:(eg + 1) * 4, s * 128:(s + 1) * 128]
                dlo = h2Tlo_v[:, eg * 4:(eg + 1) * 4, s * 128:(s + 1) * 128]
                srcv = tp[:, :512].rearrange("p (b c) -> p b c", b=4)
                W("scalar", tE, tv)
                OP("scalar", (lambda e, o=dhi, i=srcv:
                              e.activation(o, i, AF.Copy)), inc=aE)
                W("vector", aE)
                OP("vector", (lambda e, o=dlo, i=srcv, hh=dhi:
                              e.tensor_tensor(o, i, hh, ALU.subtract)), inc=vE)
                h2tp_freed[tpi] = (vE, vE.n)
                tpctr2 += 1
            h2_freed = (vE, vE.n)
        h2T_av = aE.n
        h2T_vv = vE.n

        fuse_vl = [0] * 8
        for ig in range(8):
            for il in range(8):
                i = ig * 8 + il
                gslot = i % 3
                WP("gpsimd", wgfree[gslot])
                DMA("gpsimd", wgb[gslot][:, :],
                    wg_d[i * 128:(i + 1) * 128, :], wgsm[gslot])
                WP("gpsimd", wufree[gslot])
                DMA("gpsimd", wub[gslot][:, :],
                    wu8_d[i * 128:(i + 1) * 128, :], wusm[gslot])
                W("tensor", wgsm[gslot])
                W("tensor", wusm[gslot])
                W("tensor", aE, h2T_av)
                W("tensor", vE, h2T_vv)
                PGU = TPB[i % 2]
                WP("tensor", gu_freed[i % 2])
                nmm = 0
                for col0, wring in ((0, wgb[gslot]), (t2, wub[gslot])):
                    for amat in (h2Thi_v, h2Tlo_v):
                        for e_c in range(16):
                            nmm += 1
                            OP("tensor",
                               (lambda e, o=PGU[:, col0:col0 + t2],
                                lh=wring[:, e_c * 128:(e_c + 1) * 128],
                                rh=amat[:, e_c, :],
                                st=(nmm == 1), sp=(nmm == 64):
                                e.matmul(o, lh, rh, start=st, stop=sp)),
                               inc=(tE if nmm == 64 else None))
                gu_v = tE.n
                wgfree[gslot] = (tE, gu_v)
                wufree[gslot] = (tE, gu_v)
                W("scalar", tE, gu_v)
                WP("scalar", gsb_freed)
                OP("scalar", (lambda e, s1=sg_c[:, i:i + 1], p=PGU[:, 0:t2]:
                              e.activation(gsb[:, :], p, AF.Silu,
                                           bias=0.0, scale=s1)), inc=aE)
                OP("scalar", (lambda e, s1=su_c[:, i:i + 1],
                              p=PGU[:, t2:2 * t2]:
                              e.activation(usb[:, :], p,
                                           AF.Copy, bias=0.0, scale=s1)),
                   inc=aE)
                gu_ev = aE.n
                gu_freed[i % 2] = (aE, gu_ev)
                W("vector", aE, gu_ev)
                if il == 0:
                    WP("vector", fuse_freed)
                OP("vector", lambda e: e.tensor_tensor(
                    ffb[:, :], gsb[:, :], usb[:, :], ALU.mult))
                gsb_freed = (vE, vE.n)
                OP("vector", (lambda e, o=fghi[:, il * t2:(il + 1) * t2]:
                              e.tensor_copy(o, ffb[:, :])))
                OP("vector", (lambda e, o=fglo[:, il * t2:(il + 1) * t2],
                              hh=fghi[:, il * t2:(il + 1) * t2]:
                              e.tensor_tensor(o, ffb[:, :], hh, ALU.subtract)),
                   inc=vE)
                fuse_vl[il] = vE.n
            fuse_v = vE.n
            for half in range(2):
                for il in range(8):
                    i = ig * 8 + il
                    dslot = (ig * 16 + half * 8 + il) % 6
                    WP("gpsimd", wdfree[dslot])
                    DMA("gpsimd", wdb[dslot][:, :],
                        wd_d[(i * 2 + half) * 128:(i * 2 + half + 1) * 128, :],
                        wdsm[dslot])
                    W("tensor", wdsm[dslot])
                    W("tensor", vE, fuse_vl[il])
                    if il == 0:
                        WP("tensor", pd_down_freed)
                        if ig == 0 and half == 0:
                            for bkf in wo_bank_freed:
                                WP("tensor", bkf)
                    for sub in range(nsub2):
                        for fmat in (fghi, fglo):
                            for sl2 in range(2):
                                OP("tensor",
                                   (lambda e,
                                    o=PD[:, (sub * 2 + sl2) * 512:
                                         (sub * 2 + sl2 + 1) * 512],
                                    lh=fmat[:, il * t2 + sub * 128:
                                            il * t2 + (sub + 1) * 128],
                                    rh=wdb[dslot][:, sl2 * 512:
                                                  (sl2 + 1) * 512],
                                    st=(il == 0 and fmat is fghi),
                                    sp=(il == 7 and fmat is fglo):
                                    e.matmul(o, lh, rh, start=st, stop=sp)),
                                   inc=(tE if (fmat is fglo and
                                               sub == nsub2 - 1 and
                                               sl2 == 1) else None))
                    wdfree[dslot] = (tE, tE.n)
                dn_v = tE.n
                if half == 1:
                    fuse_freed = (tE, dn_v)
                W("vector", tE, dn_v)
                for sub in range(nsub2):
                    src = PD[:, sub * 1024:(sub + 1) * 1024]
                    dst = accf[:, sub * E_ + half * 1024:
                               sub * E_ + half * 1024 + 1024]
                    if ig == 0:
                        OP("vector", (lambda e, o=dst, i=src:
                                      e.tensor_copy(o, i)),
                           inc=(vE if sub == nsub2 - 1 else None))
                    else:
                        OP("vector", (lambda e, o=dst, i=src:
                                      e.tensor_tensor(o, o, i, ALU.add)),
                           inc=(vE if sub == nsub2 - 1 else None))
                pd_down_freed = (vE, vE.n)
        for sub in range(nsub2):
            acs = accf[:, sub * E_:(sub + 1) * E_]
            xs = xcur[:, sub * E_:(sub + 1) * E_]
            OP("vector", (lambda e, o=acs, a=acs:
                          e.tensor_tensor(o, a, sdr_b[:, :], ALU.mult)))
            OP("vector", (lambda e, o=xs, a=xs, b=acs:
                          e.tensor_tensor(o, a, b, ALU.add)), inc=vE)
            fin_v = vE.n
            W("gpsimd", vE, fin_v)
            DMA("gpsimd", y_d[toff + sub * 128: toff + (sub + 1) * 128, :],
                xs, ystp[xslot])
        x2free[xslot] = (vE, vE.n)
        xst_free[xslot] = (ystp[xslot], ystp[xslot].n)

    W("gpsimd", sem["yst"])
    W("gpsimd", sem["yst1"])

    with nc.Block() as block:

        @block.sync
        def _(e):
            for f in ops["sync"]:
                f(e)

        @block.tensor
        def _(e):
            for f in ops["tensor"]:
                f(e)

        @block.vector
        def _(e):
            for f in ops["vector"]:
                f(e)

        @block.scalar
        def _(e):
            for f in ops["scalar"]:
                f(e)

        @block.gpsimd
        def _(e):
            for f in ops["gpsimd"]:
                f(e)

    p2.close()
    stack.close()
    return nc



def _host_arrays(inputs, tok):
    f32 = np.float32
    Wqk = np.asarray(inputs["L0_attn_Wqk"], f32)
    Wv = np.asarray(inputs["L0_attn_Wv"], f32)
    Wu = np.asarray(inputs["L0_attn_Wu"], f32)
    Wo = np.asarray(inputs["L0_attn_Wo"], f32)
    wg = np.asarray(inputs["L0_ffn_w_gate"])
    wu8 = np.asarray(inputs["L0_ffn_w_up"])
    wd = np.asarray(inputs["L0_ffn_w_down"])

    def attn_w(Wm):
        return np.ascontiguousarray(
            Wm.reshape(16, 128, 16, 128).transpose(2, 1, 0, 3).reshape(E_, H_))

    def colvec(b):
        return np.ascontiguousarray(np.asarray(b, f32).reshape(16, 128).T)

    d = {
        "wqk": attn_w(Wqk), "wv": attn_w(Wv), "wu": attn_w(Wu),
        "wo": np.ascontiguousarray(
            Wo.reshape(16, 128, 4, 512).transpose(2, 0, 1, 3).reshape(-1, 512)),
        "bqk": colvec(inputs["L0_attn_bqk"]),
        "bv": colvec(inputs["L0_attn_bv"]),
        "bu": colvec(inputs["L0_attn_bu"]),
        "bo": np.ascontiguousarray(
            np.asarray(inputs["L0_attn_bo"], f32).reshape(1, E_)),
        "lng": colvec(inputs["L0_attn_ln_g"]),
        "lnb": colvec(inputs["L0_attn_ln_b"]),
        "wg": np.ascontiguousarray(
            wg.reshape(16, 128, 64, 128).transpose(2, 1, 0, 3).reshape(INTER, E_)),
        "wu8": np.ascontiguousarray(
            wu8.reshape(16, 128, 64, 128).transpose(2, 1, 0, 3).reshape(INTER, E_)),
        "wd": np.ascontiguousarray(
            wd.reshape(64, 128, 2, 1024).transpose(0, 2, 1, 3).reshape(-1, 1024)),
        "sg": np.ascontiguousarray(
            np.asarray(inputs["L0_ffn_s_gate"], f32).reshape(64, 128).T),
        "su": np.ascontiguousarray(
            np.asarray(inputs["L0_ffn_s_up"], f32).reshape(64, 128).T),
        "sdr": np.ascontiguousarray(np.broadcast_to(
            np.asarray(inputs["L0_ffn_s_down"], f32).reshape(1, E_),
            (128, E_))),
        "ident": np.eye(128, dtype=f32),
        "onec": np.ones((128, 1), f32),
        "oner": np.ones((1, 128), f32),
    }
    return d


def kernel(**inputs):
    tok = None
    x = np.ascontiguousarray(np.asarray(inputs["L0_attn_x"], np.float32))
    B, N, _ = x.shape
    flat = x.reshape(B * N, E_)
    tok = (B * N) // NCORES
    shared = _host_arrays(inputs, tok)
    nc = build_nc(tok, min(512, tok), 256)
    in_maps = []
    for c in range(NCORES):
        m = dict(shared)
        m["x"] = np.ascontiguousarray(flat[c * tok:(c + 1) * tok])
        in_maps.append(m)
    res = run_bass_kernel_spmd(nc, in_maps, core_ids=list(range(NCORES)))
    out = np.concatenate([res.results[c]["y"] for c in range(NCORES)], axis=0)
    return out.reshape(B, N, E_)


# version 25 (final)
# speedup vs baseline: 1.0599x; 1.0248x over previous
import numpy as np
import concourse.bass as bass
import concourse.mybir as mybir
from concourse.bass_utils import run_bass_kernel_spmd
from contextlib import ExitStack

F32 = mybir.dt.float32
F32R = mybir.dt.float32r
BF16 = mybir.dt.bfloat16
I8 = mybir.dt.int8
AF = mybir.ActivationFunctionType
ALU = mybir.AluOpType

E_ = 2048
H_ = 2048
NH = 16
INTER = 8192
EPS = 1e-5
NCORES = 8

ENGS = ("sync", "tensor", "vector", "scalar", "gpsimd")


class Sem:
    def __init__(self, h):
        self.h = h
        self.n = 0

    def inc_(self, k):
        self.n += k
        return self.n


def build_nc(tok, p1_half, t2):
    assert tok % p1_half == 0 and tok % t2 == 0
    assert p1_half in (256, 512) and t2 % 128 == 0 and 256 <= t2 <= 512
    n_half = tok // p1_half
    n_t2 = tok // t2
    nsub1 = p1_half // 128
    nsub2 = t2 // 128
    nw1 = min(512, p1_half)
    nsl1 = p1_half // nw1
    ntb1 = p1_half // 128

    nc = bass.Bass("TRN2", target_bir_lowering=False, debug=False)

    x_d = nc.dram_tensor("x", [tok, E_], F32, kind="ExternalInput").ap()
    wqk_d = nc.dram_tensor("wqk", [E_, H_], F32, kind="ExternalInput").ap()
    wv_d = nc.dram_tensor("wv", [E_, H_], F32, kind="ExternalInput").ap()
    wu_d = nc.dram_tensor("wu", [E_, H_], F32, kind="ExternalInput").ap()
    wo_d = nc.dram_tensor("wo", [4 * 16 * 128, 512], F32, kind="ExternalInput").ap()
    bqk_d = nc.dram_tensor("bqk", [128, 16], F32, kind="ExternalInput").ap()
    bv_d = nc.dram_tensor("bv", [128, 16], F32, kind="ExternalInput").ap()
    bu_d = nc.dram_tensor("bu", [128, 16], F32, kind="ExternalInput").ap()
    bo_d = nc.dram_tensor("bo", [1, E_], F32, kind="ExternalInput").ap()
    lng_d = nc.dram_tensor("lng", [128, 16], F32, kind="ExternalInput").ap()
    lnb_d = nc.dram_tensor("lnb", [128, 16], F32, kind="ExternalInput").ap()
    wg_d = nc.dram_tensor("wg", [INTER, E_], I8, kind="ExternalInput").ap()
    wu8_d = nc.dram_tensor("wu8", [INTER, E_], I8, kind="ExternalInput").ap()
    wd_d = nc.dram_tensor("wd", [INTER * 2, 1024], I8, kind="ExternalInput").ap()
    sg_d = nc.dram_tensor("sg", [128, 64], F32, kind="ExternalInput").ap()
    su_d = nc.dram_tensor("su", [128, 64], F32, kind="ExternalInput").ap()
    sdr_d = nc.dram_tensor("sdr", [128, E_], F32, kind="ExternalInput").ap()
    id_d = nc.dram_tensor("ident", [128, 128], F32, kind="ExternalInput").ap()
    onec_d = nc.dram_tensor("onec", [128, 1], F32, kind="ExternalInput").ap()
    oner_d = nc.dram_tensor("oner", [1, 128], F32, kind="ExternalInput").ap()

    wop_d = nc.dram_tensor("wop", [4 * 16 * 128, 1024], BF16).ap()
    qsp_d = nc.dram_tensor("qsp", [H_, tok], F32).ap()
    usp_d = nc.dram_tensor("usp", [H_, tok], F32).ap()
    kvin_d = nc.dram_tensor("kvin", [128, NH * 128], F32)
    kvout_d = nc.dram_tensor("kvout", [128, NH * 128], F32)

    y_d = nc.dram_tensor("y", [tok, E_], F32, kind="ExternalOutput").ap()

    ops = {e: [] for e in ENGS}

    def E(eng, fn):
        ops[eng].append(fn)

    def W(eng, s, val=None):
        v = s.n if val is None else val
        if v > 0:
            E(eng, lambda e, h=s.h, v=v: e.wait_ge(h, v))

    def WP(eng, pair):
        if pair is not None:
            W(eng, pair[0], pair[1])

    def OP(eng, build, inc=None, k=1, dma=False):
        if not dma and eng == "vector":
            assert inc is None or inc is vE
            W(eng, vE, vE.n)
            inc = vE
        elif not dma and eng == "scalar":
            assert inc is None or inc is aE
            W(eng, aE, aE.n)
            inc = aE
        if inc is not None:
            inc.inc_(k)
            E(eng, lambda e, b=build, h=inc.h, k=k: b(e).then_inc(h, k))
        else:
            E(eng, lambda e, b=build: b(e))

    def DMA(eng, out, in_, s):
        OP(eng, (lambda e, o=out, i=in_: e.dma_start(out=o, in_=i)), inc=s,
           k=16, dma=True)

    stack = ExitStack()
    semnames = ("tE vE aE const x0 x1 wf0 wf1 wh0 wh1 spq spu kvb cc "
                "xq0 xq1 uq0 uq1 wof0 wof1 woh0 woh1 wgs0 wgs1 wus0 wus1 "
                "wds0 wds1 wds2 wds3 wds4 wds5 xq2 xq3 uq2 uq3 x2c "
                "yst x20 x21 spw0 spw1 gE "
                "wol0 wol1 wol2 wgs2 wus2 wohx2 wohx3 wohx4 wohx5 "
                "wolx3 wolx4 wolx5 wpa0 wpa1 wpb0 wpb1 yst1 x2b").split()
    sem = {n: Sem(stack.enter_context(nc.semaphore(n))) for n in semnames}
    tE, vE, aE = sem["tE"], sem["vE"], sem["aE"]

    PD = stack.enter_context(nc.psum_tensor("PD", [128, 2048], F32))
    PG = stack.enter_context(nc.psum_tensor("PG", [128, 512], F32))
    PU = stack.enter_context(nc.psum_tensor("PU", [128, 512], F32))
    PX = stack.enter_context(nc.psum_tensor("PX", [128, 512], F32))
    PY = stack.enter_context(nc.psum_tensor("PY", [128, 512], F32))
    TPB = (PG, PU)

    def sb(name, shape, dt, st=None):
        return (st or stack).enter_context(nc.sbuf_tensor(name, shape, dt))

    ident = sb("identb", [128, 128], F32)
    onec = sb("onecb", [128, 1], F32)
    oner = sb("onerb", [1, 128], F32)
    bqk_c = sb("bqkc", [128, 16], F32)
    bv_c = sb("bvc", [128, 16], F32)
    bu_c = sb("buc", [128, 16], F32)
    lng_c = sb("lngc", [128, 16], F32)
    lnb_c = sb("lnbc", [128, 16], F32)
    bo_r = sb("bor", [1, E_], F32)
    sg_c = sb("sgc", [128, 64], F32)
    su_c = sb("suc", [128, 64], F32)
    sdr_b = sb("sdrb", [128, E_], F32)
    kv_sb = sb("kvsb", [128, NH * 128], F32)

    sc = sem["const"]
    for dst, src in ((ident, id_d), (onec, onec_d), (oner, oner_d),
                     (bqk_c, bqk_d), (bv_c, bv_d), (bu_c, bu_d),
                     (lng_c, lng_d), (lnb_c, lnb_d), (bo_r, bo_d),
                     (sg_c, sg_d), (su_c, su_d), (sdr_b, sdr_d)):
        DMA("sync", dst[:, :], src, sc)
    for eng in ("tensor", "vector", "scalar"):
        W(eng, sc)

    p1 = ExitStack()
    n_h1buf = 2
    h1T_hi = [sb("h1Thi%d" % b, [128, 16 * p1_half], BF16, p1)
              for b in range(n_h1buf)]
    h1T_lo = [sb("h1Tlo%d" % b, [128, 16 * p1_half], BF16, p1)
              for b in range(n_h1buf)]
    h1Thi_vb = [t[:, :].rearrange("p (e t) -> p e t", e=16) for t in h1T_hi]
    h1Tlo_vb = [t[:, :].rearrange("p (e t) -> p e t", e=16) for t in h1T_lo]
    xb1 = [sb("xb1%d" % i, [128, E_], F32, p1) for i in range(3)]
    h1b = [sb("h1b%d" % i, [128, E_], F32, p1) for i in range(3)]
    j1 = sb("j1", [128, E_], F32, p1)
    j2 = sb("j2", [128, E_], F32, p1)
    statc = sb("statc", [128, 8], F32, p1)
    wfb = [sb("wf0b", [128, E_], F32, p1), sb("wf1b", [128, E_], F32, p1)]
    whb = [sb("wh0b", [128, E_], BF16, p1), sb("wh1b", [128, E_], BF16, p1)]
    wlb = [sb("wl0b", [128, E_], BF16, p1), sb("wl1b", [128, E_], BF16, p1)]
    wpf = [sb("wpf0", [128, 512], F32, p1)]
    wph = [sb("wph0", [128, 512], BF16, p1)]
    wpl = [sb("wpl0", [128, 512], BF16, p1)]
    qfm = sb("qfm", [128, p1_half], F32, p1)
    vfm = sb("vfm", [128, p1_half], F32, p1)
    ufm = sb("ufm", [128, p1_half], F32, p1)
    qtm = sb("qtm", [128, p1_half], F32, p1)
    vtm = sb("vtm", [128, p1_half], F32, p1)
    tmpa = sb("tmpa", [128, p1_half], F32, p1)
    tmpb = sb("tmpb", [128, p1_half], F32, p1)

    xsem = [sem["x0"], sem["x1"], sem["x2b"]]
    xfree = [None, None, None]
    wfsem = [sem["wf0"], sem["wf1"]]
    whsem = [sem["wh0"], sem["wh1"]]
    wfree = [None, None]
    wo_j = [0]

    wpfree = [None, None]

    def wo_presplit(n_slabs):
        for _ in range(n_slabs):
            j = wo_j[0]
            if j >= 64:
                return
            wo_j[0] += 1
            slot = 0
            if wpfree[slot] is not None:
                W("scalar", *wpfree[slot])
                W("gpsimd", *wpfree[slot])
            srcw = wo_d[j * 128:(j + 1) * 128, :]
            wpa = sem["wpa0"]
            wpb = sem["wpb0"]
            DMA("scalar", wpf[slot][:, :], srcw, wpa)
            DMA("gpsimd", wph[slot][:, :], srcw, wpb)
            W("gpsimd", wpa)
            W("gpsimd", wpb)
            OP("gpsimd", (lambda e, o=wpl[slot][:, :],
                          a=wpf[slot][:, :],
                          b=wph[slot][:, :]:
                          e.tensor_tensor(o, a, b, ALU.subtract)),
               inc=sem["gE"])
            W("gpsimd", sem["gE"], sem["gE"].n)
            spw = sem["spw0"]
            DMA("gpsimd", wop_d[j * 128:(j + 1) * 128, 0:512],
                wph[slot][:, :], spw)
            DMA("gpsimd", wop_d[j * 128:(j + 1) * 128, 512:1024],
                wpl[slot][:, :], spw)
            wpfree[slot] = (spw, spw.n)
    tp_freed = [None, None]
    tp_ctr = [0]
    bank_freed = [None, None, None, None]
    px_freed = [None]
    kv_seeded = [False] * NH
    h1_freed = [None, None, None]

    h1T_freed = [None, None]
    h1T_ready = [0, 0]

    h1_ready = {}

    def a_step_stats(half, s):
        off = half * p1_half
        slot = (half * nsub1 + s) % 3
        if xfree[slot] is not None:
            W("sync", *xfree[slot])
        DMA("sync", xb1[slot][:, :],
            x_d[off + s * 128: off + (s + 1) * 128, :], xsem[slot])
        xv = xsem[slot].n
        W("vector", xsem[slot], xv)
        OP("vector", (lambda e, i=xb1[slot][:, :]:
                      e.tensor_scalar(j1[:, :], i, 1.0, 0.0, ALU.mult,
                                      ALU.add,
                                      accum_out=statc[:, 0:1])), inc=vE)
        W("vector", vE)
        W("scalar", xsem[slot], xv)
        OP("scalar", (lambda e, i=xb1[slot][:, :]:
                      e.activation(j2[:, :], i, AF.Square,
                                   accum_out=statc[:, 1:2])), inc=aE)
        W("vector", aE)
        OP("vector", lambda e: e.tensor_scalar(
            statc[:, 2:3], statc[:, 0:1], 1.0 / E_, None, ALU.mult))
        OP("vector", lambda e: e.tensor_tensor(
            statc[:, 3:4], statc[:, 2:3], statc[:, 2:3], ALU.mult))
        OP("vector", lambda e: e.tensor_scalar(
            statc[:, 4:5], statc[:, 1:2], 1.0 / E_, None, ALU.mult))
        OP("vector", lambda e: e.tensor_tensor(
            statc[:, 3:4], statc[:, 4:5], statc[:, 3:4], ALU.subtract))
        OP("vector", lambda e: e.tensor_scalar(
            statc[:, 3:4], statc[:, 3:4], EPS, None, ALU.add))
        OP("vector", lambda e: e.reciprocal(
            statc[:, 4:5], statc[:, 3:4]), inc=vE)
        W("scalar", vE)
        OP("scalar", lambda e: e.activation(
            statc[:, 4:5], statc[:, 4:5], AF.Sqrt), inc=aE)
        hb = h1b[slot]
        WP("vector", h1_freed[slot])
        W("vector", aE)
        OP("vector", (lambda e, i=xb1[slot][:, :], o=hb[:, :]:
                      e.tensor_scalar(o, i, statc[:, 2:3], statc[:, 4:5],
                                      ALU.subtract, ALU.mult)), inc=vE)
        h1_v = vE.n
        xfree[slot] = (vE, h1_v)
        h1_ready[(half, s)] = h1_v

    def a_step_tp(half, s):
        buf = half % n_h1buf
        slot = (half * nsub1 + s) % 3
        hb = h1b[slot]
        h1_v = h1_ready[(half, s)]
        for eg in range(4):
            tpi = tp_ctr[0] % 2
            tp = TPB[tpi]
            W("tensor", vE, h1_v)
            WP("tensor", tp_freed[tpi])
            for k in range(4):
                OP("tensor",
                   (lambda e, o=tp[:, k * 128:(k + 1) * 128],
                    d=hb[:, (eg * 4 + k) * 128:(eg * 4 + k + 1) * 128],
                    st=(k == 0), sp=(k == 3):
                    e.matmul(o, d, ident[:, :], start=st, stop=sp,
                             is_transpose=True)),
                   inc=(tE if k == 3 else None))
            tv = tE.n
            dhi = h1Thi_vb[buf][:, eg * 4:(eg + 1) * 4, s * 128:(s + 1) * 128]
            dlo = h1Tlo_vb[buf][:, eg * 4:(eg + 1) * 4, s * 128:(s + 1) * 128]
            srcv = tp[:, :512].rearrange("p (b c) -> p b c", b=4)
            W("scalar", tE, tv)
            WP("scalar", h1T_freed[buf])
            OP("scalar", (lambda e, o=dhi, i=srcv:
                          e.activation(o, i, AF.Copy)), inc=aE)
            av = aE.n
            W("vector", aE, av)
            OP("vector", (lambda e, o=dlo, i=srcv, h2=dhi:
                          e.tensor_tensor(o, i, h2, ALU.subtract)), inc=vE)
            tp_freed[tpi] = (vE, vE.n)
            tp_ctr[0] += 1
        h1_freed[slot] = (vE, vE.n)
        if s == nsub1 - 1:
            h1T_ready[buf] = vE.n
        lo_h, hi_h = (1, min(3, n_half - 1)) if n_half > 1 else (0, 0)
        if lo_h <= half <= hi_h:
            nsl_tot = (hi_h - lo_h + 1) * nsub1
            wo_presplit((64 // nsl_tot) +
                        (64 % nsl_tot if (half == hi_h and s == nsub1 - 1)
                         else 0))

    def a_step_sub(half, s):
        a_step_stats(half, s)
        a_step_tp(half, s)

    def w_loop(half):
        buf = half % n_h1buf
        off = half * p1_half
        h1Thi_v = h1Thi_vb[buf]
        h1Tlo_v = h1Tlo_vb[buf]
        projs = (("q", wqk_d, bqk_c), ("v", wv_d, bv_c), ("u", wu_d, bu_c))

        def w_load(gl):
            m2, pi2 = divmod(gl, 3)
            slot = (half * 48 + gl) % 2
            wsrc = projs[pi2][1]
            if wfree[slot] is not None:
                W("sync", *wfree[slot])
                W("gpsimd", *wfree[slot])
            DMA("sync", wfb[slot][:, :],
                wsrc[m2 * 128:(m2 + 1) * 128, :], wfsem[slot])
            DMA("gpsimd", whb[slot][:, :],
                wsrc[m2 * 128:(m2 + 1) * 128, :], whsem[slot])

        def w_sub(gl):
            slot = (half * 48 + gl) % 2
            W("vector", wfsem[slot])
            W("vector", whsem[slot])
            OP("vector", (lambda e, o=wlb[slot][:, :], a=wfb[slot][:, :],
                          b=whb[slot][:, :]:
                          e.tensor_tensor(o, a, b, ALU.subtract)), inc=vE)
            sub_v[slot] = vE.n

        sub_v = [0, 0]
        w_load(0)
        w_load(1)
        w_sub(0)
        eps = {}
        hooks = {}
        if half + 1 < n_half:
            for s2 in range(nsub1):
                g0 = min(10 + s2 * max(1, 28 // nsub1), 38)
                hooks.setdefault(g0, []).append(("stats", half + 1, s2))
                hooks.setdefault(g0 + 4, []).append(("tp", half + 1, s2))
        for gl in range(48):
            for kind2, h2_, s2_ in hooks.get(gl, ()):
                if kind2 == "stats":
                    a_step_stats(h2_, s2_)
                else:
                    a_step_tp(h2_, s2_)
            m, pi = divmod(gl, 3)
            kind, wsrc, bias = projs[pi]
            g = half * 48 + gl
            slot = g % 2
            bank = g % 4
            pp = PD[:, bank * 512: bank * 512 + 512]
            if gl + 1 < 48:
                w_sub(gl + 1)
            W("tensor", vE, max(sub_v[slot], h1T_ready[buf]))
            WP("tensor", bank_freed[bank])
            first = True
            for p_i, (arhs, wmat) in enumerate(
                    ((h1Thi_v, whb[slot]), (h1Tlo_v, whb[slot]),
                     (h1Thi_v, wlb[slot]))):
                for e_c in range(16):
                    for ns in range(nsl1):
                        last = (p_i == 2 and e_c == 15 and ns == nsl1 - 1)
                        OP("tensor",
                           (lambda e, o=pp[:, ns * nw1:(ns + 1) * nw1],
                            lh=wmat[:, e_c * 128:(e_c + 1) * 128],
                            rh=arhs[:, e_c, ns * nw1:(ns + 1) * nw1],
                            st=first, sp=last:
                            e.matmul(o, lh, rh, start=st, stop=sp)),
                           inc=(tE if last else None))
                        first = False
            mm_v = tE.n
            wfree[slot] = (tE, mm_v)
            if gl + 2 < 48:
                w_load(gl + 2)
            if kind == "q":
                W("vector", tE, mm_v)
                W("vector", sem["spq"], sem["spq"].n)
                OP("vector", (lambda e, b=bias[:, m:m + 1], p=pp:
                              e.tensor_scalar(tmpa[:, :], p[:, :p1_half],
                                              b, None, ALU.add)))
                OP("vector", lambda e: e.tensor_scalar(
                    tmpb[:, :], tmpa[:, :], 0.0, None, ALU.min), inc=vE)
                W("scalar", vE)
                OP("scalar", lambda e: e.activation(
                    tmpb[:, :], tmpb[:, :], AF.Exp), inc=aE)
                OP("vector", lambda e: e.tensor_scalar(
                    qfm[:, :], tmpa[:, :], 0.0, -1.0, ALU.max, ALU.add))
                W("vector", aE)
                OP("vector", lambda e: e.tensor_tensor(
                    qfm[:, :], qfm[:, :], tmpb[:, :], ALU.add), inc=vE)
                ep = (vE, vE.n)
                bank_freed[bank] = ep
                W("gpsimd", vE, ep[1])
                DMA("gpsimd",
                    qsp_d[m * 128:(m + 1) * 128, off:off + p1_half],
                    qfm[:, :], sem["spq"])
                eps["q"] = ep
            elif kind == "v":
                W("scalar", tE, mm_v)
                OP("scalar", (lambda e, b=bias[:, m:m + 1], p=pp:
                              e.activation(vfm[:, :], p[:, :p1_half],
                                           AF.Silu, bias=b, scale=1.0)),
                   inc=aE)
                ep = (aE, aE.n)
                bank_freed[bank] = ep
                eps["v"] = ep
            else:
                W("scalar", tE, mm_v)
                W("scalar", sem["spu"], sem["spu"].n)
                OP("scalar", (lambda e, b=bias[:, m:m + 1], p=pp:
                              e.activation(ufm[:, :], p[:, :p1_half],
                                           AF.Silu, bias=b, scale=1.0)),
                   inc=aE)
                ep = (aE, aE.n)
                bank_freed[bank] = ep
                W("gpsimd", aE, ep[1])
                DMA("gpsimd",
                    usp_d[m * 128:(m + 1) * 128, off:off + p1_half],
                    ufm[:, :], sem["spu"])
                for src_fm, dst_tm, epk in ((qfm, qtm, "q"), (vfm, vtm, "v")):
                    for tb0 in range(0, ntb1, 4):
                        nblk = min(4, ntb1 - tb0)
                        tpi = tp_ctr[0] % 2
                        tp = TPB[tpi]
                        WP("tensor", eps[epk])
                        WP("tensor", tp_freed[tpi])
                        for k in range(nblk):
                            OP("tensor",
                               (lambda e, o=tp[:, k * 128:(k + 1) * 128],
                                d=src_fm[:, (tb0 + k) * 128:
                                         (tb0 + k + 1) * 128],
                                st=(k == 0), sp=(k == nblk - 1):
                                e.matmul(o, d, ident[:, :], start=st, stop=sp,
                                         is_transpose=True)),
                               inc=(tE if k == nblk - 1 else None))
                        tv = tE.n
                        W("vector", tE, tv)
                        OP("vector",
                           (lambda e, o=dst_tm[:, tb0 * 128:
                                               tb0 * 128 + nblk * 128],
                            i=tp[:, :nblk * 128]:
                            e.tensor_copy(o, i)), inc=vE)
                        tp_freed[tpi] = (vE, vE.n)
                kv_in_v = vE.n
                W("tensor", vE, kv_in_v)
                WP("tensor", px_freed[0])
                for b in range(ntb1):
                    OP("tensor",
                       (lambda e, lh=qtm[:, b * 128:(b + 1) * 128],
                        rh=vtm[:, b * 128:(b + 1) * 128],
                        st=(b == 0), sp=(b == ntb1 - 1):
                        e.matmul(PX[:, :128], lh, rh, start=st, stop=sp)),
                       inc=(tE if b == ntb1 - 1 else None))
                kvmm_v = tE.n
                W("vector", tE, kvmm_v)
                if not kv_seeded[m]:
                    OP("vector", (lambda e, o=kv_sb[:, m * 128:(m + 1) * 128]:
                                  e.tensor_copy(o, PX[:, :128])), inc=vE)
                    kv_seeded[m] = True
                else:
                    OP("vector", (lambda e, o=kv_sb[:, m * 128:(m + 1) * 128]:
                                  e.tensor_tensor(o, o, PX[:, :128], ALU.add)),
                       inc=vE)
                px_freed[0] = (vE, vE.n)
        h1T_freed[buf] = (tE, tE.n)

    for s0 in range(nsub1):
        a_step_sub(0, s0)
    for half in range(n_half):
        w_loop(half)
        if n_h1buf == 1 and half + 1 < n_half:
            for s0 in range(nsub1):
                if (half + 1, s0) in h1_ready:
                    a_step_tp(half + 1, s0)
                else:
                    a_step_sub(half + 1, s0)

    spq_done = sem["spq"].n
    spu_done = sem["spu"].n
    spw_done = (sem["spw0"].n, sem["spw1"].n)
    assert wo_j[0] == 64
    W("gpsimd", vE)
    DMA("gpsimd", kvin_d.ap(), kv_sb[:, :], sem["kvb"])
    W("gpsimd", sem["kvb"])
    OP("gpsimd", lambda e: e.collective_compute(
        "AllReduce", ALU.add, replica_groups=[[0, 1, 2, 3], [4, 5, 6, 7]],
        ins=[kvin_d.ap()], outs=[kvout_d.ap()]), inc=sem["cc"])
    W("gpsimd", sem["cc"])
    DMA("gpsimd", kv_sb[:, :], kvout_d.ap(), sem["kvb"])
    kv_final = sem["kvb"].n

    p1t, p1v, p1a = tE.n, vE.n, aE.n
    for eng in ENGS:
        W(eng, tE, p1t)
        W(eng, vE, p1v)
        W(eng, aE, p1a)
        W(eng, sem["spq"], spq_done)
        W(eng, sem["spu"], spu_done)
        W(eng, sem["spw0"], spw_done[0])
        W(eng, sem["spw1"], spw_done[1])
        W(eng, sem["kvb"], kv_final)

    p1.close()

    p2 = ExitStack()
    xb2 = [sb("xb2a", [128, nsub2 * E_], F32, p2),
           sb("xb2b", [128, nsub2 * E_], F32, p2)]
    qhb = [sb("qh%d" % i, [128, t2], F32, p2) for i in range(4)]
    uhb = [sb("uh%d" % i, [128, t2], F32, p2) for i in range(4)]
    outsb = sb("outsb", [128, NH * t2], F32, p2)
    sq2 = [sb("sqb%d" % i, [128, t2], F32, p2) for i in range(2)]
    sqb = sq2[0]
    rowb = sb("rowb", [1, 5 * t2], F32, p2)
    afb2 = [sb("afb%d" % i, [128, t2], F32, p2) for i in range(2)]
    af22 = [sqb, sb("af2b", [128, t2], F32, p2)]
    arena_hi = sb("arenahi", [128, NH * t2], BF16, p2)
    arena_lo = sb("arenalo", [128, NH * t2], BF16, p2)
    h2T_hi_t = sb("h2thi", [128, NH * t2], BF16, p2)
    h2T_lo_t = sb("h2tlo", [128, NH * t2], BF16, p2)
    h2Thi_v = h2T_hi_t[:, :].rearrange("p (e t) -> p e t", e=16)
    h2Tlo_v = h2T_lo_t[:, :].rearrange("p (e t) -> p e t", e=16)
    wopb = [sb("wop%d" % i, [128, 1024], BF16, p2) for i in range(6)]
    h2b = sb("h2b", [128, E_], F32, p2)
    st2 = sb("st2", [128, 8], F32, p2)
    wgb = [sb("wg%db" % i, [128, E_], BF16, p2) for i in range(3)]
    wub = [sb("wu%db" % i, [128, E_], BF16, p2) for i in range(3)]
    wdb = [sb("wd%db" % i, [128, 1024], BF16, p2) for i in range(6)]
    gsb = sb("gsb", [128, t2], F32, p2)
    usb = sb("usb", [128, t2], F32, p2)
    ffb = sb("ffb", [128, t2], F32, p2)
    fghi = sb("fghi", [128, 8 * t2], BF16, p2)
    fglo = sb("fglo", [128, 8 * t2], BF16, p2)
    assert NH * t2 == nsub2 * E_
    accf = outsb

    x2sem = [sem["x20"], sem["x21"]]
    qsem = [sem["xq0"], sem["xq1"], sem["xq2"], sem["xq3"]]
    usem = [sem["uq0"], sem["uq1"], sem["uq2"], sem["uq3"]]
    wos4 = [sem["wof0"], sem["wof1"], sem["woh0"], sem["woh1"],
            sem["wohx2"], sem["wohx3"]]
    wgsm = [sem["wgs0"], sem["wgs1"], sem["wgs2"]]
    wusm = [sem["wus0"], sem["wus1"], sem["wus2"]]
    wdsm = [sem["wds0"], sem["wds1"], sem["wds2"], sem["wds3"],
            sem["wds4"], sem["wds5"]]

    qfree = [None] * 4
    ufree = [None] * 4
    wofree = [None] * 6
    wgfree = [None, None, None]
    wufree = [None, None, None]
    wdfree = [None] * 6
    x2free = [None, None]
    xst_free = [None, None]
    ystp = [sem["yst"], sem["yst1"]]
    oev_freed = [None, None]
    wo_bank_freed = [None] * 4
    pd_down_freed = None
    fuse_freed = None
    gu_freed = [None, None]
    h2tp_freed = [None, None]
    stats_freed = None
    h2_freed = None


    qctr = 0
    woctr = 0
    tpctr2 = 0
    gsb_freed = None
    ffb_freed = None

    for t in range(n_t2):
        toff = t * t2
        xslot = t % 2
        xcur = xb2[xslot]
        WP("sync", x2free[xslot])
        WP("sync", xst_free[xslot])
        for s in range(nsub2):
            DMA("sync", xcur[:, s * E_:(s + 1) * E_],
                x_d[toff + s * 128: toff + (s + 1) * 128, :], x2sem[xslot])
        xv2 = x2sem[xslot].n

        WP("tensor", stats_freed)
        WP("tensor", gu_freed[0])
        WP("tensor", gu_freed[1])
        WP("tensor", h2tp_freed[0])
        WP("tensor", h2tp_freed[1])
        W("scalar", vE, vE.n)
        def emit_stats(h):
            sv = sq_vh[h]
            W("tensor", aE, sv)
            OP("tensor", (lambda e, rh=outsb[:, h * t2:(h + 1) * t2],
                          st=(h == 0), sp=(h == NH - 1):
                          e.matmul(PX[:1, :t2], onec[:, :], rh,
                                   start=st, stop=sp)),
               inc=(tE if h == NH - 1 else None))
            OP("tensor", (lambda e, st=(h == 0), sp=(h == NH - 1),
                          sqs=sq2[h % 2]:
                          e.matmul(PY[:1, :t2], onec[:, :], sqs[:, :],
                                   start=st, stop=sp)),
               inc=(tE if h == NH - 1 else None))

        sq_vh = [0] * NH
        for h in range(NH):
            slot = qctr % 4
            WP("sync", qfree[slot])
            DMA("sync", qhb[slot][:, :],
                qsp_d[h * 128:(h + 1) * 128, toff:toff + t2], qsem[slot])
            qv = qsem[slot].n
            tpi = h % 2
            tp = TPB[tpi]
            W("tensor", qsem[slot], qv)
            WP("tensor", oev_freed[tpi])
            OP("tensor", (lambda e, lh=kv_sb[:, h * 128:(h + 1) * 128],
                          rh=qhb[slot][:, :], o=tp[:, :t2]:
                          e.matmul(o, lh, rh, start=True, stop=True)), inc=tE)
            ov = tE.n
            qfree[slot] = (tE, ov)
            W("scalar", tE, ov)
            OP("scalar", (lambda e, o=outsb[:, h * t2:(h + 1) * t2],
                          i=tp[:, :t2]: e.activation(o, i, AF.Copy)), inc=aE)
            OP("scalar", (lambda e, i=tp[:, :t2], sqs=sq2[h % 2]:
                          e.activation(sqs[:, :], i, AF.Square)), inc=aE)
            sq_vh[h] = aE.n
            oev_freed[tpi] = (aE, aE.n)
            qctr += 1
            if h >= 1:
                emit_stats(h - 1)
        emit_stats(NH - 1)
        stats_v = tE.n
        R = rowb
        W("vector", tE, stats_v)
        OP("vector", lambda e: e.tensor_scalar(
            R[:1, 0 * t2:1 * t2], PX[:1, :t2], 1.0 / H_, None, ALU.mult))
        OP("vector", lambda e: e.tensor_scalar(
            R[:1, 1 * t2:2 * t2], PY[:1, :t2], 1.0 / H_, None, ALU.mult))
        OP("vector", lambda e: e.tensor_tensor(
            R[:1, 2 * t2:3 * t2], R[:1, 0 * t2:1 * t2], R[:1, 0 * t2:1 * t2],
            ALU.mult))
        OP("vector", lambda e: e.tensor_tensor(
            R[:1, 2 * t2:3 * t2], R[:1, 1 * t2:2 * t2], R[:1, 2 * t2:3 * t2],
            ALU.subtract))
        OP("vector", lambda e: e.tensor_scalar(
            R[:1, 2 * t2:3 * t2], R[:1, 2 * t2:3 * t2], EPS, None, ALU.add))
        OP("vector", lambda e: e.reciprocal(
            R[:1, 3 * t2:4 * t2], R[:1, 2 * t2:3 * t2]), inc=vE)
        W("scalar", vE)
        OP("scalar", lambda e: e.activation(
            R[:1, 3 * t2:4 * t2], R[:1, 3 * t2:4 * t2], AF.Sqrt), inc=aE)
        W("vector", aE)
        OP("vector", lambda e: e.tensor_tensor(
            R[:1, 4 * t2:5 * t2], R[:1, 0 * t2:1 * t2], R[:1, 3 * t2:4 * t2],
            ALU.mult), inc=vE)
        rows_v = vE.n
        W("tensor", vE, rows_v)
        OP("tensor", lambda e: e.matmul(
            PX[:, :t2], oner[:, :], R[:1, 3 * t2:4 * t2],
            start=True, stop=True), inc=tE)
        OP("tensor", lambda e: e.matmul(
            PY[:, :t2], oner[:, :], R[:1, 4 * t2:5 * t2],
            start=True, stop=True), inc=tE)
        bc_v = tE.n

        a_vh = [0] * NH
        afb_freed = [None, None]
        for h in range(NH):
            slot = h % 4
            WP("scalar", ufree[slot])
            DMA("scalar", uhb[slot][:, :],
                usp_d[h * 128:(h + 1) * 128, toff:toff + t2], usem[slot])
            uv = usem[slot].n
            afb = afb2[h % 2]
            af2 = af22[h % 2]
            W("vector", tE, bc_v)
            W("vector", usem[slot], uv)
            WP("vector", afb_freed[h % 2])
            OP("vector", (lambda e, i=outsb[:, h * t2:(h + 1) * t2], o=afb:
                          e.tensor_tensor(o[:, :], i, PX[:, :t2], ALU.mult)))
            OP("vector", (lambda e, o=afb: e.tensor_tensor(
                o[:, :], o[:, :], PY[:, :t2], ALU.subtract)), inc=vE)
            W("scalar", vE, vE.n)
            OP("scalar", (lambda e, gg=lng_c[:, h:h + 1], bb=lnb_c[:, h:h + 1],
                          o=af2, i=afb:
                          e.activation(o[:, :], i[:, :], AF.Identity,
                                       bias=bb, scale=gg)), inc=aE)
            W("vector", aE, aE.n)
            OP("vector", (lambda e, u=uhb[slot][:, :], o=af2:
                          e.tensor_tensor(o[:, :], o[:, :], u, ALU.mult)),
               inc=vE)
            ufree[slot] = (vE, vE.n)
            W("scalar", vE, vE.n)
            OP("scalar", (lambda e, o=arena_hi[:, h * t2:(h + 1) * t2], i=af2:
                          e.activation(o, i[:, :], AF.Copy)), inc=aE)
            afb_hi_a = aE.n
            W("vector", aE, aE.n)
            OP("vector", (lambda e, o=arena_lo[:, h * t2:(h + 1) * t2],
                          hh=arena_hi[:, h * t2:(h + 1) * t2],
                          i=af2:
                          e.tensor_tensor(o, i[:, :], hh, ALU.subtract)),
               inc=vE)
            a_vh[h] = vE.n
            afb_freed[h % 2] = (vE, vE.n)
        a_v = vE.n
        stats_freed = (vE, a_v)

        for p in range(2):
            for h in range(NH):
                for sl in (2 * p, 2 * p + 1):
                    slot = woctr % 6
                    WP("sync", wofree[slot])
                    DMA("sync", wopb[slot][:, :],
                        wop_d[(sl * 16 + h) * 128:(sl * 16 + h + 1) * 128, :],
                        wos4[slot])
                    W("tensor", wos4[slot])
                    W("tensor", vE, a_vh[h])
                    if h == 0:
                        WP("tensor", pd_down_freed)
                        WP("tensor", wo_bank_freed[(sl % 2) * 2])
                        WP("tensor", wo_bank_freed[(sl % 2) * 2 + 1])
                    for sub in range(nsub2):
                        bank = (sl % 2) * nsub2 + sub
                        pp = PD[:, bank * 512:(bank + 1) * 512]
                        if h == 0:
                            OP("tensor", (lambda e, o=pp,
                                          rh=bo_r[:1, sl * 512:(sl + 1) * 512]:
                                          e.matmul(o, oner[:, :], rh,
                                                   start=True, stop=False)))
                        for pi2, (amat, wmat) in enumerate(
                                ((arena_hi, wopb[slot][:, 0:512]),
                                 (arena_lo, wopb[slot][:, 0:512]),
                                 (arena_hi, wopb[slot][:, 512:1024]))):
                            sp = (h == NH - 1 and pi2 == 2)
                            OP("tensor",
                               (lambda e, o=pp,
                                lh=amat[:, h * t2 + sub * 128:
                                        h * t2 + (sub + 1) * 128],
                                rh=wmat[:, :], sp=sp:
                                e.matmul(o, lh, rh, start=False, stop=sp)),
                               inc=(tE if (pi2 == 2 and
                                           sub == nsub2 - 1) else None))
                    wofree[slot] = (tE, tE.n)
                    woctr += 1
            grp_v = tE.n
            W("vector", tE, grp_v)
            W("vector", x2sem[xslot], xv2)
            for sl in (2 * p, 2 * p + 1):
                for sub in range(nsub2):
                    bank = (sl % 2) * nsub2 + sub
                    pp = PD[:, bank * 512:(bank + 1) * 512]
                    xs = xcur[:, sub * E_ + sl * 512:
                              sub * E_ + (sl + 1) * 512]
                    OP("vector", (lambda e, o=xs, a=xs, b=pp:
                                  e.tensor_tensor(o, a, b, ALU.add)), inc=vE)
                    wo_bank_freed[bank] = (vE, vE.n)
        xnew_v = vE.n
        x2free[xslot] = (vE, xnew_v)

        for s in range(nsub2):
            xs = xcur[:, s * E_:(s + 1) * E_]
            W("vector", vE, xnew_v)
            OP("vector", (lambda e, i=xs:
                          e.tensor_scalar(accf[:, 0:E_], i, 1.0, 0.0,
                                          ALU.mult, ALU.add,
                                          accum_out=st2[:, 0:1])), inc=vE)
            W("vector", vE)
            W("scalar", vE, xnew_v)
            OP("scalar", (lambda e, i=xs:
                          e.activation(accf[:, E_:2 * E_], i, AF.Square,
                                       accum_out=st2[:, 1:2])), inc=aE)
            W("vector", aE)
            OP("vector", lambda e: e.tensor_scalar(
                st2[:, 2:3], st2[:, 0:1], 1.0 / E_, None, ALU.mult))
            OP("vector", lambda e: e.tensor_tensor(
                st2[:, 3:4], st2[:, 2:3], st2[:, 2:3], ALU.mult))
            OP("vector", lambda e: e.tensor_scalar(
                st2[:, 4:5], st2[:, 1:2], 1.0 / E_, None, ALU.mult))
            OP("vector", lambda e: e.tensor_tensor(
                st2[:, 3:4], st2[:, 4:5], st2[:, 3:4], ALU.subtract))
            OP("vector", lambda e: e.tensor_scalar(
                st2[:, 3:4], st2[:, 3:4], EPS, None, ALU.add))
            OP("vector", lambda e: e.reciprocal(
                st2[:, 4:5], st2[:, 3:4]), inc=vE)
            W("scalar", vE)
            OP("scalar", lambda e: e.activation(
                st2[:, 4:5], st2[:, 4:5], AF.Sqrt), inc=aE)
            WP("vector", h2_freed)
            W("vector", aE)
            OP("vector", (lambda e, i=xs:
                          e.tensor_scalar(h2b[:, :], i, st2[:, 2:3],
                                          st2[:, 4:5], ALU.subtract,
                                          ALU.mult)), inc=vE)
            h2_v = vE.n
            for eg in range(4):
                tpi = tpctr2 % 2
                tp = TPB[tpi]
                W("tensor", vE, h2_v)
                WP("tensor", h2tp_freed[tpi])
                WP("tensor", oev_freed[tpi])
                oev_freed[tpi] = None
                WP("tensor", gu_freed[tpi])
                gu_freed[tpi] = None
                for k in range(4):
                    OP("tensor",
                       (lambda e, o=tp[:, k * 128:(k + 1) * 128],
                        d=h2b[:, (eg * 4 + k) * 128:(eg * 4 + k + 1) * 128],
                        st=(k == 0), sp=(k == 3):
                        e.matmul(o, d, ident[:, :], start=st, stop=sp,
                                 is_transpose=True)),
                       inc=(tE if k == 3 else None))
                tv = tE.n
                dhi = h2Thi_v[:, eg * 4:(eg + 1) * 4, s * 128:(s + 1) * 128]
                dlo = h2Tlo_v[:, eg * 4:(eg + 1) * 4, s * 128:(s + 1) * 128]
                srcv = tp[:, :512].rearrange("p (b c) -> p b c", b=4)
                W("scalar", tE, tv)
                OP("scalar", (lambda e, o=dhi, i=srcv:
                              e.activation(o, i, AF.Copy)), inc=aE)
                W("vector", aE)
                OP("vector", (lambda e, o=dlo, i=srcv, hh=dhi:
                              e.tensor_tensor(o, i, hh, ALU.subtract)), inc=vE)
                h2tp_freed[tpi] = (vE, vE.n)
                tpctr2 += 1
            h2_freed = (vE, vE.n)
        h2T_av = aE.n
        h2T_vv = vE.n

        fuse_vl = [0] * 8
        dn_bank_freed = [None] * 4
        for ig in range(8):
            for il in range(8):
                i = ig * 8 + il
                gslot = i % 3
                WP("gpsimd", wgfree[gslot])
                DMA("gpsimd", wgb[gslot][:, :],
                    wg_d[i * 128:(i + 1) * 128, :], wgsm[gslot])
                WP("gpsimd", wufree[gslot])
                DMA("gpsimd", wub[gslot][:, :],
                    wu8_d[i * 128:(i + 1) * 128, :], wusm[gslot])
                W("tensor", wgsm[gslot])
                W("tensor", wusm[gslot])
                W("tensor", aE, h2T_av)
                W("tensor", vE, h2T_vv)
                PGU = TPB[i % 2]
                WP("tensor", gu_freed[i % 2])
                nmm = 0
                for col0, wring in ((0, wgb[gslot]), (t2, wub[gslot])):
                    for amat in (h2Thi_v, h2Tlo_v):
                        for e_c in range(16):
                            nmm += 1
                            OP("tensor",
                               (lambda e, o=PGU[:, col0:col0 + t2],
                                lh=wring[:, e_c * 128:(e_c + 1) * 128],
                                rh=amat[:, e_c, :],
                                st=(nmm == 1), sp=(nmm == 64):
                                e.matmul(o, lh, rh, start=st, stop=sp)),
                               inc=(tE if nmm == 64 else None))
                gu_v = tE.n
                wgfree[gslot] = (tE, gu_v)
                wufree[gslot] = (tE, gu_v)
                W("scalar", tE, gu_v)
                WP("scalar", gsb_freed)
                OP("scalar", (lambda e, s1=sg_c[:, i:i + 1], p=PGU[:, 0:t2]:
                              e.activation(gsb[:, :], p, AF.Silu,
                                           bias=0.0, scale=s1)), inc=aE)
                OP("scalar", (lambda e, s1=su_c[:, i:i + 1],
                              p=PGU[:, t2:2 * t2]:
                              e.activation(usb[:, :], p,
                                           AF.Copy, bias=0.0, scale=s1)),
                   inc=aE)
                gu_ev = aE.n
                gu_freed[i % 2] = (aE, gu_ev)
                W("vector", aE, gu_ev)
                if il == 0:
                    WP("vector", fuse_freed)
                OP("vector", lambda e: e.tensor_tensor(
                    ffb[:, :], gsb[:, :], usb[:, :], ALU.mult))
                gsb_freed = (vE, vE.n)
                OP("vector", (lambda e, o=fghi[:, il * t2:(il + 1) * t2]:
                              e.tensor_copy(o, ffb[:, :])))
                OP("vector", (lambda e, o=fglo[:, il * t2:(il + 1) * t2],
                              hh=fghi[:, il * t2:(il + 1) * t2]:
                              e.tensor_tensor(o, ffb[:, :], hh, ALU.subtract)),
                   inc=vE)
                fuse_vl[il] = vE.n
            fuse_v = vE.n
            for half in range(2):
                for il in range(8):
                    i = ig * 8 + il
                    dslot = (ig * 16 + half * 8 + il) % 6
                    WP("gpsimd", wdfree[dslot])
                    DMA("gpsimd", wdb[dslot][:, :],
                        wd_d[(i * 2 + half) * 128:(i * 2 + half + 1) * 128, :],
                        wdsm[dslot])
                    W("tensor", wdsm[dslot])
                    W("tensor", vE, fuse_vl[il])
                    for sub in range(nsub2):
                        for fmat in (fghi, fglo):
                            for sl2 in range(2):
                                bank = sub * 2 + sl2
                                if il == 0 and fmat is fghi:
                                    WP("tensor", dn_bank_freed[bank])
                                    if ig == 0 and half == 0:
                                        WP("tensor",
                                           wo_bank_freed[bank])
                                OP("tensor",
                                   (lambda e,
                                    o=PD[:, (sub * 2 + sl2) * 512:
                                         (sub * 2 + sl2 + 1) * 512],
                                    lh=fmat[:, il * t2 + sub * 128:
                                            il * t2 + (sub + 1) * 128],
                                    rh=wdb[dslot][:, sl2 * 512:
                                                  (sl2 + 1) * 512],
                                    st=(il == 0 and fmat is fghi),
                                    sp=(il == 7 and fmat is fglo):
                                    e.matmul(o, lh, rh, start=st, stop=sp)),
                                   inc=(tE if (fmat is fglo and
                                               sub == nsub2 - 1 and
                                               sl2 == 1) else None))
                    wdfree[dslot] = (tE, tE.n)
                dn_v = tE.n
                if half == 1:
                    fuse_freed = (tE, dn_v)
                W("vector", tE, dn_v)
                for sub in range(nsub2):
                    for sl2 in range(2):
                        bank = sub * 2 + sl2
                        srcp = PD[:, bank * 512:(bank + 1) * 512]
                        dst = accf[:, sub * E_ + half * 1024 + sl2 * 512:
                                   sub * E_ + half * 1024 + (sl2 + 1) * 512]
                        if ig == 0:
                            OP("vector", (lambda e, o=dst, i=srcp:
                                          e.tensor_copy(o, i)), inc=vE)
                        else:
                            OP("vector", (lambda e, o=dst, i=srcp:
                                          e.tensor_tensor(o, o, i, ALU.add)),
                               inc=vE)
                        dn_bank_freed[bank] = (vE, vE.n)
                pd_down_freed = (vE, vE.n)
        for sub in range(nsub2):
            acs = accf[:, sub * E_:(sub + 1) * E_]
            xs = xcur[:, sub * E_:(sub + 1) * E_]
            OP("vector", (lambda e, o=acs, a=acs:
                          e.tensor_tensor(o, a, sdr_b[:, :], ALU.mult)))
            OP("vector", (lambda e, o=xs, a=xs, b=acs:
                          e.tensor_tensor(o, a, b, ALU.add)), inc=vE)
            fin_v = vE.n
            W("gpsimd", vE, fin_v)
            DMA("gpsimd", y_d[toff + sub * 128: toff + (sub + 1) * 128, :],
                xs, ystp[xslot])
        x2free[xslot] = (vE, vE.n)
        xst_free[xslot] = (ystp[xslot], ystp[xslot].n)

    W("gpsimd", sem["yst"])
    W("gpsimd", sem["yst1"])

    with nc.Block() as block:

        @block.sync
        def _(e):
            for f in ops["sync"]:
                f(e)

        @block.tensor
        def _(e):
            for f in ops["tensor"]:
                f(e)

        @block.vector
        def _(e):
            for f in ops["vector"]:
                f(e)

        @block.scalar
        def _(e):
            for f in ops["scalar"]:
                f(e)

        @block.gpsimd
        def _(e):
            for f in ops["gpsimd"]:
                f(e)

    p2.close()
    stack.close()
    return nc



def _host_arrays(inputs, tok):
    f32 = np.float32
    Wqk = np.asarray(inputs["L0_attn_Wqk"], f32)
    Wv = np.asarray(inputs["L0_attn_Wv"], f32)
    Wu = np.asarray(inputs["L0_attn_Wu"], f32)
    Wo = np.asarray(inputs["L0_attn_Wo"], f32)
    wg = np.asarray(inputs["L0_ffn_w_gate"])
    wu8 = np.asarray(inputs["L0_ffn_w_up"])
    wd = np.asarray(inputs["L0_ffn_w_down"])

    def attn_w(Wm):
        return np.ascontiguousarray(
            Wm.reshape(16, 128, 16, 128).transpose(2, 1, 0, 3).reshape(E_, H_))

    def colvec(b):
        return np.ascontiguousarray(np.asarray(b, f32).reshape(16, 128).T)

    d = {
        "wqk": attn_w(Wqk), "wv": attn_w(Wv), "wu": attn_w(Wu),
        "wo": np.ascontiguousarray(
            Wo.reshape(16, 128, 4, 512).transpose(2, 0, 1, 3).reshape(-1, 512)),
        "bqk": colvec(inputs["L0_attn_bqk"]),
        "bv": colvec(inputs["L0_attn_bv"]),
        "bu": colvec(inputs["L0_attn_bu"]),
        "bo": np.ascontiguousarray(
            np.asarray(inputs["L0_attn_bo"], f32).reshape(1, E_)),
        "lng": colvec(inputs["L0_attn_ln_g"]),
        "lnb": colvec(inputs["L0_attn_ln_b"]),
        "wg": np.ascontiguousarray(
            wg.reshape(16, 128, 64, 128).transpose(2, 1, 0, 3).reshape(INTER, E_)),
        "wu8": np.ascontiguousarray(
            wu8.reshape(16, 128, 64, 128).transpose(2, 1, 0, 3).reshape(INTER, E_)),
        "wd": np.ascontiguousarray(
            wd.reshape(64, 128, 2, 1024).transpose(0, 2, 1, 3).reshape(-1, 1024)),
        "sg": np.ascontiguousarray(
            np.asarray(inputs["L0_ffn_s_gate"], f32).reshape(64, 128).T),
        "su": np.ascontiguousarray(
            np.asarray(inputs["L0_ffn_s_up"], f32).reshape(64, 128).T),
        "sdr": np.ascontiguousarray(np.broadcast_to(
            np.asarray(inputs["L0_ffn_s_down"], f32).reshape(1, E_),
            (128, E_))),
        "ident": np.eye(128, dtype=f32),
        "onec": np.ones((128, 1), f32),
        "oner": np.ones((1, 128), f32),
    }
    return d


def kernel(**inputs):
    tok = None
    x = np.ascontiguousarray(np.asarray(inputs["L0_attn_x"], np.float32))
    B, N, _ = x.shape
    flat = x.reshape(B * N, E_)
    tok = (B * N) // NCORES
    shared = _host_arrays(inputs, tok)
    nc = build_nc(tok, min(512, tok), 256)
    in_maps = []
    for c in range(NCORES):
        m = dict(shared)
        m["x"] = np.ascontiguousarray(flat[c * tok:(c + 1) * tok])
        in_maps.append(m)
    res = run_bass_kernel_spmd(nc, in_maps, core_ids=list(range(NCORES)))
    out = np.concatenate([res.results[c]["y"] for c in range(NCORES)], axis=0)
    return out.reshape(B, N, E_)
